# version 15
# baseline (speedup 1.0000x reference)
import math
import numpy as np
import ml_dtypes
import concourse.bass as bass
import concourse.mybir as mybir
from concourse.bass_utils import run_bass_kernel_spmd

F32 = mybir.dt.float32
BF16 = mybir.dt.bfloat16
AF = mybir.ActivationFunctionType
ALU = mybir.AluOpType

NH = 16
HD = 128
E = 2048
HFF = 8192
NPAIR = 64
ENGINES = ("tensor", "vector", "scalar", "gpsimd", "sync")


def _slopes_r():
    start = 2.0 ** (-(2.0 ** (-(math.log2(NH) - 3))))
    slopes = np.array([start * start ** i for i in range(NH)], dtype=np.float32)
    return np.exp(-slopes).astype(np.float32)


class Ctr:
    __slots__ = ("h", "n", "name")

    def __init__(self, h, name):
        self.h = h
        self.n = 0
        self.name = name

    def reserve(self, k=1):
        self.n += k
        return self.n

    def tok(self):
        return (self, self.n)


class KB:
    def __init__(self, nc):
        self.nc = nc
        self.items = []
        self.ctrs = {}
        self.last_dve_w = frozenset()
        self._dve_item_end = 0
        self.pe = self.ctr("c_pe")
        self.dve = self.ctr("c_dve")
        self.act = self.ctr("c_act")
        self.pool = self.ctr("c_pool")

    def ctr(self, name):
        if name not in self.ctrs:
            self.ctrs[name] = Ctr(self.nc.alloc_semaphore(name), name)
        return self.ctrs[name]

    def emit(self, eng, fn, wait=(), inc=(), rd=(), wr=()):
        incs = []
        tok = None
        for c, k in inc:
            tok = c.reserve(k)
            incs.append((c.h, k))
        if eng == "vector":
            rdids = {id(x) for x in rd} | {id(x) for x in wr}
            if rdids & self.last_dve_w:
                wait = list(wait) + [(self.dve, self._dve_item_end)]
            self.last_dve_w = frozenset(id(x) for x in wr)
            self._dve_item_end = self.dve.n
        self.items.append((eng, tuple((c.h, v) for c, v in wait), fn, tuple(incs), False))
        return tok

    def dma_batch(self, eng, ctr, fns, wait=()):
        ctr.reserve(16 * len(fns))
        h = ctr.h

        def f(e, fns=tuple(fns), h=h):
            last = None
            for g in fns:
                last = g(e)
                last.then_inc(h, 16)
            return last
        self.items.append((eng, tuple((c.h, v) for c, v in wait), f, (), False))
        return ctr.n

    def barrier(self):
        bar = self.ctr("c_bar")
        own = {"tensor": self.pe, "vector": self.dve, "scalar": self.act, "gpsimd": self.pool}
        for eng in ENGINES:
            w = [(own[eng], own[eng].n)] if eng in own else []
            self.emit(eng, lambda e: e.nop(), wait=w, inc=[(bar, 1)])
        target = bar.n
        for eng in ENGINES:
            self.emit(eng, None, wait=[(bar, target)])
        self.last_dve_w = frozenset()

    def replay(self):
        nc = self.nc
        items = self.items
        with nc.Block() as block:
            for engname, deco in (("tensor", block.tensor), ("vector", block.vector),
                                  ("scalar", block.scalar), ("gpsimd", block.gpsimd),
                                  ("sync", block.sync)):
                def make(engname):
                    def run(e):
                        for (eng, waits, fn, incs, hz) in items:
                            if eng != engname:
                                continue
                            for (h, v) in waits:
                                e.wait_ge(h, v)
                            inst = fn(e) if fn is not None else None
                            if incs:
                                for j, (h, k) in enumerate(incs):
                                    if j == 0:
                                        inst.then_inc(h, k)
                                    else:
                                        e.nop().then_inc(h, k)
                    return run
                deco(make(engname))


def build_nc(G, dbg=False, phases=None):
    Tc = G * 128
    nc = bass.Bass("TRN2", target_bir_lowering=False, debug=False)
    K = KB(nc)
    sb = nc.alloc_sbuf_tensor
    ps = nc.alloc_psum_tensor
    rC = [float(x) for x in np.power(_slopes_r(), np.float32(128.0)).astype(np.float32)]

    d_x = nc.dram_tensor("x", [Tc, E], F32, kind="ExternalInput").ap()
    d_wq = nc.dram_tensor("wq", [E, E], F32, kind="ExternalInput").ap()
    d_wk = nc.dram_tensor("wk", [E, E], F32, kind="ExternalInput").ap()
    d_wv = nc.dram_tensor("wv", [E, E], F32, kind="ExternalInput").ap()
    d_wo = nc.dram_tensor("wo", [E, E], F32, kind="ExternalInput").ap()
    d_rmsw = nc.dram_tensor("rmsw", [128, 16], F32, kind="ExternalInput").ap()
    d_wguT = nc.dram_tensor("wguT", [E, 2 * HFF], F32, kind="ExternalInput").ap()
    d_wpT = nc.dram_tensor("wpT", [HFF, E], F32, kind="ExternalInput").ap()
    d_cos = nc.dram_tensor("costab", [G, 128, 256], F32, kind="ExternalInput").ap()
    d_sin = nc.dram_tensor("sintab", [G, 128, 256], F32, kind="ExternalInput").ap()
    d_kdec = nc.dram_tensor("kdecay", [128, E], F32, kind="ExternalInput").ap()
    d_dmT = nc.dram_tensor("dmaskT", [16, 128, 128], F32, kind="ExternalInput").ap()
    d_rdec = nc.dram_tensor("rdecay", [128, 16], F32, kind="ExternalInput").ap()
    d_wvec = nc.dram_tensor("wvec", [128, 8, 16], F32, kind="ExternalInput").ap()
    d_E2 = nc.dram_tensor("E2", [128, 2], BF16, kind="ExternalInput").ap()
    d_B2 = nc.dram_tensor("B2", [2, 128], BF16, kind="ExternalInput").ap()
    d_id32 = nc.dram_tensor("id32", [128, 128], F32, kind="ExternalInput").ap()
    d_idbf = nc.dram_tensor("idbf", [128, 128], BF16, kind="ExternalInput").ap()

    d_out = nc.dram_tensor("out", [Tc, E], F32, kind="ExternalOutput").ap()
    sk = "ExternalOutput" if dbg else "Internal"
    d_qfm = nc.dram_tensor("qfm_s", [G, 128, E], BF16, kind=sk).ap()
    d_kfm = nc.dram_tensor("kfm_s", [G, 128, E], BF16, kind=sk).ap()
    d_kp = nc.dram_tensor("kp_s", [G, 128, E], BF16, kind=sk).ap()
    d_vs = nc.dram_tensor("v_s", [G, 128, E], BF16, kind=sk).ap()
    d_x2T = nc.dram_tensor("x2T_s", [16, 128, Tc], BF16, kind=sk).ap()
    d_hs = nc.dram_tensor("h_s", [NPAIR, 128, Tc], BF16, kind=sk).ap()
    d_wpq = nc.dram_tensor("wpq_s", [64, 128, E], BF16, kind=sk).ap()
    d_kvl = nc.dram_tensor("kvl", [128, E], F32, kind=sk).ap() if dbg else None
    d_kvin = nc.dram_tensor("kvin", [128, E], F32, kind=sk).ap() if dbg else None
    d_ccin = nc.dram_tensor("cc_in", [NH * HD, HD], BF16, kind="Internal")
    d_ccout = nc.dram_tensor("cc_out", [8 * NH * HD, HD], BF16, kind="Internal")

    t_E2 = sb("t_E2", [128, 2], BF16)
    t_B2 = sb("t_B2", [2, 128], BF16)
    t_id32 = sb("t_id32", [128, 128], F32)
    t_idbf = sb("t_idbf", [128, 128], BF16)
    t_eps5 = sb("t_eps5", [128, 1], F32)
    t_epsf = sb("t_epsf", [1, 1], F32)
    t_i8k = sb("t_i8k", [1, 1], F32)
    t_onebf = sb("t_onebf", [128, 1], BF16)
    t_onef = sb("t_onef", [1, 1], F32)
    t_two = sb("t_two", [128, 1], F32)
    t_nhalf = sb("t_nhalf", [128, 1], F32)
    t_nh1 = sb("t_nh1", [1, 1], F32)
    base_sbuf = nc.sbuf_base
    base_psum = nc.psum_base
    t_kv = sb("t_kv", [128, E], F32)
    t_kvbf = sb("t_kvbf", [128, E], BF16)
    base_kv = nc.sbuf_base

    ld0 = K.ctr("ld_const")
    K.dma_batch("sync", ld0, [
        lambda e: e.dma_start(out=t_E2[:], in_=d_E2[:, :]),
        lambda e: e.dma_start(out=t_B2[:], in_=d_B2[:, :]),
        lambda e: e.dma_start(out=t_id32[:], in_=d_id32[:, :]),
        lambda e: e.dma_start(out=t_idbf[:], in_=d_idbf[:, :]),
    ])

    def f_msets(e):
        e.memset(t_eps5[:], 1e-5)
        e.memset(t_epsf[:], float(np.finfo(np.float32).eps))
        e.memset(t_i8k[:], 1.0 / 8192.0)
        e.memset(t_onebf[:], 1.0)
        e.memset(t_onef[:], 1.0)
        e.memset(t_two[:], 2.0)
        e.memset(t_nhalf[:], -0.5)
        e.memset(t_nh1[:], -0.5)
        return e.memset(t_kv[:], 0.0)
    K.emit("vector", f_msets, inc=[(K.dve, 1)], wr=[t_kv])
    w0 = [ld0.tok(), K.dve.tok()]
    for eng in ENGINES:
        K.emit(eng, None, wait=w0)

    t_xlnT = sb("t_xlnT", [128, 16, Tc], BF16)
    epoch_sbuf = nc.sbuf_base

    def p0():
        nc.sbuf_base = epoch_sbuf
        nc.psum_base = base_psum
        xg = [sb("p0_xg0", [128, E], F32), sb("p0_xg1", [128, E], F32)]
        xl = [sb("p0_xl0", [128, E], BF16), sb("p0_xl1", [128, E], BF16)]
        stats = sb("p0_stats", [128, 24], F32)
        mv = sb("p0_mv", [128, 2], F32)
        srt = sb("p0_srt", [128, 1], F32)
        rstd = sb("p0_rstd", [128, 1], F32)
        nmb = sb("p0_nmb", [128, 1], F32)
        pt = [ps("p0_pt0", [128, 1024], BF16), ps("p0_pt1", [128, 1024], BF16)]
        c_ldx = [K.ctr("p0_ldx0"), K.ctr("p0_ldx1")]
        tok_apply, tok_tr, tok_ev = {}, {}, {}
        for g in range(G):
            p = g % 2
            w = [(K.act, tok_apply[g - 2])] if g - 2 in tok_apply else []
            K.emit("sync", lambda e, p=p, g=g: e.dma_start(out=xg[p][:], in_=d_x[g * 128:(g + 1) * 128, :]),
                   wait=w, inc=[(c_ldx[p], 16)])
            v1, v2 = K.dve.reserve(1), K.dve.reserve(1)
            def f_stats(e, p=p, v1=v1, v2=v2):
                st = None
                for c4 in range(4):
                    st = e.bn_stats(stats[:, 6 * c4:6 * c4 + 6], xg[p][:, 512 * c4:512 * (c4 + 1)])
                st.then_inc(K.dve.h, 1)
                e.wait_ge(K.dve.h, v1)
                e.bn_aggr(mv[:], stats[:, 0:24]).then_inc(K.dve.h, 1)
                e.wait_ge(K.dve.h, v2)
                return e.tensor_scalar_mul(nmb[:], mv[:, 0:1], -1.0)
            tv = K.emit("vector", f_stats, wait=[c_ldx[p].tok()], inc=[(K.dve, 1)],
                        rd=[xg[p]], wr=[nmb, mv, stats])
            va = K.act.reserve(1)
            def f_rstd(e, va=va):
                e.activation(srt[:], mv[:, 1:2], AF.Ln, bias=t_eps5[:]).then_inc(K.act.h, 1)
                e.wait_ge(K.act.h, va)
                return e.activation(rstd[:], srt[:], AF.Exp, scale=t_nhalf[:, 0:1])
            ta = K.emit("scalar", f_rstd, wait=[(K.dve, tv)], inc=[(K.act, 1)])
            tv2 = K.emit("vector", lambda e: e.tensor_mul(nmb[:], nmb[:], rstd[:]),
                         wait=[(K.act, ta)], inc=[(K.dve, 1)], rd=[rstd, nmb], wr=[nmb])
            w = [(K.dve, tv2)] + ([(K.pe, tok_tr[g - 2])] if g - 2 in tok_tr else [])
            tok_apply[g] = K.emit("scalar",
                lambda e, p=p: e.activation(xl[p][:], xg[p][:], AF.Identity, scale=rstd[:], bias=nmb[:]),
                wait=w, inc=[(K.act, 1)])
            def f_tr(e, p=p):
                last = None
                for c in range(16):
                    bank = pt[c // 8]
                    last = e.matmul(bank[:, (c % 8) * 128:(c % 8 + 1) * 128],
                                    xl[p][:, c * 128:(c + 1) * 128], t_idbf[:],
                                    is_transpose=True, start=(c % 8 == 0), stop=(c % 8 == 7),
                                    skip_group_check=True)
                return last
            w = [(K.act, tok_apply[g])] + ([(K.dve, tok_ev[g - 1])] if g - 1 in tok_ev else [])
            tok_tr[g] = K.emit("tensor", f_tr, wait=w, inc=[(K.pe, 1)])
            def f_ev(e, g=g):
                e.tensor_copy(t_xlnT[:, 0:8, g * 128:(g + 1) * 128],
                              pt[0].ap().rearrange("p (c t) -> p c t", c=8))
                return e.tensor_copy(t_xlnT[:, 8:16, g * 128:(g + 1) * 128],
                                     pt[1].ap().rearrange("p (c t) -> p c t", c=8))
            tok_ev[g] = K.emit("vector", f_ev, wait=[(K.pe, tok_tr[g])], inc=[(K.dve, 1)],
                               rd=[], wr=[t_xlnT])
        K.barrier()

    def p_qkv(which):
        nc.sbuf_base = epoch_sbuf
        nc.psum_base = base_psum
        W = {"q": d_wq, "k": d_wk, "v": d_wv}[which]
        w4 = sb(f"{which}_w4", [128, 16, 1024], BF16)
        pp = [[ps(f"{which}_ppA0", [128, 512], F32), ps(f"{which}_ppA1", [128, 512], F32)],
              [ps(f"{which}_ppB0", [128, 512], F32), ps(f"{which}_ppB1", [128, 512], F32)]]
        c_ldw = K.ctr(f"{which}_ldw")
        c_sp = [K.ctr(f"{which}_sp0"), K.ctr(f"{which}_sp1")]
        tok_mm, tok_ev, tok_rot, tok_tr, tok_st = {}, {}, {}, {}, {}
        if which in ("q", "k"):
            qtm = [sb(f"{which}_tm0", [128, 1024], F32), sb(f"{which}_tm1", [128, 1024], F32)]
            tmp = sb(f"{which}_tmp", [128, 4, 256], F32)
            cs = [sb(f"{which}_c0", [128, 256], F32), sb(f"{which}_c1", [128, 256], F32)]
            sn = [sb(f"{which}_s0", [128, 256], F32), sb(f"{which}_s1", [128, 256], F32)]
            tp = [ps(f"{which}_tp0", [128, 512], F32), ps(f"{which}_tp1", [128, 512], F32)]
            stg = [sb(f"{which}_sg0", [128, 1024], BF16), sb(f"{which}_sg1", [128, 1024], BF16)]
            c_cs = [K.ctr(f"{which}_cc0"), K.ctr(f"{which}_cc1")]
            if which == "k":
                t_kdec = sb("p2_kdec", [128, E], F32)
                c_kdec = K.ctr("p2_kdec_ld")
                K.emit("sync", lambda e: e.dma_start(out=t_kdec[:], in_=d_kdec[:, :]), inc=[(c_kdec, 16)])
                kpst = [sb("k_kp0", [128, 1024], BF16), sb("k_kp1", [128, 1024], BF16)]
                c_sp2 = [K.ctr("k_sp20"), K.ctr("k_sp21")]
        else:
            vst = [sb("v_st0", [128, 1024], BF16), sb("v_st1", [128, 1024], BF16)]
            kpb = [sb("v_kpb0", [128, 1024], BF16), sb("v_kpb1", [128, 1024], BF16)]
            pkv = [ps("v_pkv0", [128, 512], F32), ps("v_pkv1", [128, 512], F32)]
            c_kp = [K.ctr("v_kp0"), K.ctr("v_kp1")]
            tok_upd = {}
        Wr = W.rearrange("(kc p) n -> p kc n", p=128)
        pending = [None]
        for q2 in range(2):
            w = [K.pe.tok()] if q2 > 0 else []
            K.dma_batch("gpsimd", c_ldw, [
                (lambda e, q2=q2, k4=k4: e.dma_start(
                    out=w4[:, 4 * k4:4 * (k4 + 1), :],
                    in_=Wr[:, 4 * k4:4 * (k4 + 1), q2 * 1024:(q2 + 1) * 1024]))
                for k4 in range(4)], wait=w)
            ldwv = c_ldw.tok()
            for g in range(G):
                p = g % 2
                gi = q2 * G + g
                def f_mm(e, p=p, g=g):
                    last = None
                    for kc in range(16):
                        e.matmul(pp[p][0][:], t_xlnT[:, kc, g * 128:(g + 1) * 128],
                                 w4[:, kc, 0:512], start=(kc == 0), stop=(kc == 15))
                        last = e.matmul(pp[p][1][:], t_xlnT[:, kc, g * 128:(g + 1) * 128],
                                        w4[:, kc, 512:1024], start=(kc == 0), stop=(kc == 15))
                    return last
                w = [ldwv]
                if gi - 2 in tok_ev:
                    w.append((K.act, tok_ev[gi - 2]))
                tok_mm[gi] = K.emit("tensor", f_mm, wait=w, inc=[(K.pe, 1)])
                if which in ("q", "k"):
                    w = [(K.dve, tok_rot[gi - 2])] if gi - 2 in tok_rot else []
                    K.dma_batch("sync", c_cs[p], [
                        lambda e, p=p, g=g: e.dma_start(out=cs[p][:], in_=d_cos[g, :, :]),
                        lambda e, p=p, g=g: e.dma_start(out=sn[p][:], in_=d_sin[g, :, :])],
                        wait=w)
                    w = [(K.pe, tok_mm[gi])]
                    if gi - 2 in tok_tr:
                        w.append((K.pe, tok_tr[gi - 2]))
                    def f_ev(e, p=p):
                        e.activation(qtm[p][:, 0:512], pp[p][0][:], AF.Identity)
                        return e.activation(qtm[p][:, 512:1024], pp[p][1][:], AF.Identity)
                    tok_ev[gi] = K.emit("scalar", f_ev, wait=w, inc=[(K.act, 1)])
                    v1 = K.dve.reserve(1)
                    vh = K.dve.reserve(1) if which == "k" else None
                    def f_rot(e, p=p, q2=q2, v1=v1, vh=vh):
                        a = qtm[p].ap().rearrange("t (h d) -> t h d", h=8)
                        x1, x2 = a[:, :, 0:32], a[:, :, 64:96]
                        c4 = cs[p].ap().rearrange("t (h j) -> t h j", h=8)
                        s4 = sn[p].ap().rearrange("t (h j) -> t h j", h=8)
                        r4 = lambda v: v.rearrange("t (h j) -> t h j", h=8)
                        e.tensor_mul(r4(tmp[:, 0, :]), x1, c4)
                        e.tensor_mul(r4(tmp[:, 1, :]), x1, s4)
                        e.tensor_mul(r4(tmp[:, 2, :]), x2, s4)
                        e.tensor_mul(r4(tmp[:, 3, :]), x2, c4).then_inc(K.dve.h, 1)
                        e.wait_ge(K.dve.h, v1)
                        e.tensor_add(x1, r4(tmp[:, 0, :]), r4(tmp[:, 2, :]))
                        last = e.tensor_sub(x2, r4(tmp[:, 3, :]), r4(tmp[:, 1, :]))
                        if which == "k":
                            last.then_inc(K.dve.h, 1)
                            e.wait_ge(K.dve.h, vh)
                            last = e.tensor_mul(kpst[p][:], qtm[p][:], t_kdec[:, q2 * 1024:(q2 + 1) * 1024])
                        return last
                    w = [(K.act, tok_ev[gi]), c_cs[p].tok()]
                    if which == "k":
                        w.append(c_kdec.tok())
                        if gi - 2 in tok_rot:
                            w.append(c_sp2[p].tok())
                    tok_rot[gi] = K.emit("vector", f_rot, wait=w, inc=[(K.dve, 1)],
                                         rd=[qtm[p], cs[p], sn[p]],
                                         wr=[qtm[p], tmp] + ([kpst[p]] if which == "k" else []))
                    if which == "k":
                        K.emit("sync", lambda e, p=p, g=g, q2=q2: e.dma_start(
                            out=d_kp[g, :, q2 * 1024:(q2 + 1) * 1024], in_=kpst[p][:]),
                            wait=[(K.dve, tok_rot[gi])], inc=[(c_sp2[p], 16)])
                    def emit_tail(gi=gi, p=p, g=g, q2=q2):
                        def f_tr(e, p=p):
                            last = None
                            for j in range(8):
                                bank = tp[j // 4]
                                last = e.matmul(bank[:, (j % 4) * 128:(j % 4 + 1) * 128],
                                                qtm[p][:, j * 128:(j + 1) * 128], t_id32[:],
                                                is_transpose=True, start=(j % 4 == 0), stop=(j % 4 == 3),
                                                skip_group_check=True)
                            return last
                        w = [(K.dve, tok_rot[gi])]
                        if gi - 1 in tok_st:
                            w.append((K.dve, tok_st[gi - 1]))
                        tok_tr[gi] = K.emit("tensor", f_tr, wait=w, inc=[(K.pe, 1)])
                        w = [(K.pe, tok_tr[gi]), c_sp[p].tok()]
                        def f_st(e, p=p):
                            e.tensor_copy(stg[p][:, 0:512], tp[0][:])
                            return e.tensor_copy(stg[p][:, 512:1024], tp[1][:])
                        tok_st[gi] = K.emit("vector", f_st, wait=w, inc=[(K.dve, 1)], rd=[], wr=[stg[p]])
                        dst = d_qfm if which == "q" else d_kfm
                        K.emit("sync", lambda e, p=p, g=g, q2=q2, dst=dst: e.dma_start(
                            out=dst[g, :, q2 * 1024:(q2 + 1) * 1024], in_=stg[p][:]),
                            wait=[(K.dve, tok_st[gi])], inc=[(c_sp[p], 16)])
                    if pending[0] is not None:
                        pending[0]()
                    pending[0] = emit_tail
                else:
                    w = [(K.pe, tok_mm[gi]), c_sp[p].tok()]
                    if gi - 2 in tok_tr:
                        w.append((K.pe, tok_tr[gi - 2]))
                    def f_ev(e, p=p):
                        e.activation(vst[p][:, 0:512], pp[p][0][:], AF.Identity)
                        return e.activation(vst[p][:, 512:1024], pp[p][1][:], AF.Identity)
                    tok_ev[gi] = K.emit("scalar", f_ev, wait=w, inc=[(K.act, 1)])
                    K.emit("sync", lambda e, p=p, g=g, q2=q2: e.dma_start(
                        out=d_vs[g, :, q2 * 1024:(q2 + 1) * 1024], in_=vst[p][:]),
                        wait=[(K.act, tok_ev[gi])], inc=[(c_sp[p], 16)])
                    w = [(K.pe, tok_tr[gi - 2])] if gi - 2 in tok_tr else []
                    K.emit("sync", lambda e, p=p, g=g, q2=q2: e.dma_start(
                        out=kpb[p][:], in_=d_kp[g, :, q2 * 1024:(q2 + 1) * 1024]),
                        wait=w, inc=[(c_kp[p], 16)])
                    def emit_tail(gi=gi, p=p, q2=q2):
                        def f_kvu(e, p=p):
                            last = None
                            for j in range(8):
                                bank = pkv[j // 4]
                                last = e.matmul(bank[:, (j % 4) * 128:(j % 4 + 1) * 128],
                                                kpb[p][:, j * 128:(j + 1) * 128],
                                                vst[p][:, j * 128:(j + 1) * 128],
                                                start=(j % 4 == 0), stop=(j % 4 == 3), skip_group_check=True)
                            return last
                        w = [(K.act, tok_ev[gi]), c_kp[p].tok()]
                        if gi - 1 in tok_upd:
                            w.append((K.dve, tok_upd[gi - 1]))
                        tok_tr[gi] = K.emit("tensor", f_kvu, wait=w, inc=[(K.pe, 1)])
                        def f_upd(e, p=p, q2=q2):
                            last = None
                            for j in range(8):
                                h = q2 * 8 + j
                                last = e.scalar_tensor_tensor(
                                    t_kv[:, h * 128:(h + 1) * 128], t_kv[:, h * 128:(h + 1) * 128],
                                    rC[h], pkv[j // 4][:, (j % 4) * 128:(j % 4 + 1) * 128], ALU.mult, ALU.add)
                            return last
                        tok_upd[gi] = K.emit("vector", f_upd, wait=[(K.pe, tok_tr[gi])], inc=[(K.dve, 1)],
                                             rd=[t_kv], wr=[t_kv])
                    if pending[0] is not None:
                        pending[0]()
                    pending[0] = emit_tail
        if pending[0] is not None:
            pending[0]()
            pending[0] = None
        drains = [c_sp[0].tok(), c_sp[1].tok()]
        if which == "k":
            drains += [c_sp2[0].tok(), c_sp2[1].tok()]
        K.emit("sync", None, wait=drains)
        K.barrier()

    def px():
        nc.sbuf_base = epoch_sbuf
        t_wvec = sb("x_wvec", [128, 128], F32)
        kvg = [sb("x_kvg0", [128, E], BF16), sb("x_kvg1", [128, E], BF16)]
        c_cc = K.ctr("x_cc")
        c_xi = K.ctr("x_xi")
        c_xg = [K.ctr("x_xg0"), K.ctr("x_xg1")]
        c_db = K.ctr("x_db")
        c_xw = K.ctr("x_xw")
        K.emit("sync", lambda e: e.dma_start(out=t_wvec[:], in_=d_wvec.rearrange("p j h -> p (j h)")),
               inc=[(c_xw, 16)])
        if dbg:
            K.emit("sync", lambda e: e.dma_start(out=d_kvl[:, :], in_=t_kv[:]), inc=[(c_db, 16)])
        t_cast = K.emit("scalar", lambda e: e.activation(t_kvbf[:], t_kv[:], AF.Identity),
                        inc=[(K.act, 1)])
        K.emit("gpsimd", lambda e: e.dma_start(
            out=d_ccin.ap().rearrange("(h d) e2 -> d h e2", h=NH),
            in_=t_kvbf.ap().rearrange("p (h e2) -> p h e2", h=NH)),
            wait=[(K.act, t_cast)], inc=[(c_xi, 16)])
        K.emit("gpsimd", lambda e: e.collective_compute(
            "AllGather", ALU.bypass, replica_groups=[list(range(8))],
            ins=[d_ccin.ap().opt()], outs=[d_ccout.ap().opt()]),
            wait=[c_xi.tok()] + ([c_db.tok()] if dbg else []), inc=[(c_cc, 1)])
        toks = []
        for j in range(8):
            p = j % 2
            w = [c_cc.tok()]
            if j >= 2:
                w.append((K.dve, toks[j - 2]))
            K.emit("gpsimd", lambda e, j=j, p=p: e.dma_start(
                out=kvg[p].ap().rearrange("p2 (h e2) -> p2 h e2", h=NH),
                in_=d_ccout[j * NH * HD:(j + 1) * NH * HD, :].rearrange("(h d) e2 -> d h e2", h=NH)),
                wait=w, inc=[(c_xg[p], 16)])
            def f_comb(e, j=j, p=p):
                last = None
                for h in range(NH):
                    if j == 0:
                        last = e.tensor_scalar_mul(t_kv[:, h * 128:(h + 1) * 128],
                                                   kvg[p][:, h * 128:(h + 1) * 128],
                                                   t_wvec[:, j * 16 + h:j * 16 + h + 1])
                    else:
                        last = e.scalar_tensor_tensor(
                            t_kv[:, h * 128:(h + 1) * 128], kvg[p][:, h * 128:(h + 1) * 128],
                            t_wvec[:, j * 16 + h:j * 16 + h + 1], t_kv[:, h * 128:(h + 1) * 128],
                            ALU.mult, ALU.add)
                return last
            toks.append(K.emit("vector", f_comb, wait=[c_xg[p].tok(), c_xw.tok()], inc=[(K.dve, 1)],
                               rd=[kvg[p], t_kv], wr=[t_kv]))
        K.emit("vector", lambda e: e.tensor_copy(t_kvbf[:], t_kv[:]),
               wait=[(K.dve, toks[-1])], inc=[(K.dve, 1)], rd=[t_kv], wr=[t_kvbf])
        if dbg:
            K.emit("sync", lambda e: e.dma_start(out=d_kvin[:, :], in_=t_kv[:]),
                   wait=[K.dve.tok()], inc=[(c_db, 16)])
        K.barrier()

    def p4():
        nc.sbuf_base = base_kv
        nc.psum_base = base_psum
        wob = sb("p4_wob", [128, 16, E], BF16)
        t_dmT = sb("p4_dmT", [128, NH * 128], F32)
        t_rdec = sb("p4_rdec", [128, 16], F32)
        t_rmsw = sb("p4_rmsw", [128, 16], F32)
        qc = [sb(f"p4_q{i}", [128, E], BF16) for i in range(3)]
        kc = [sb(f"p4_k{i}", [128, E], BF16) for i in range(3)]
        kpc = [sb(f"p4_kp{i}", [128, E], BF16) for i in range(3)]
        vc = [sb(f"p4_v{i}", [128, E], BF16) for i in range(3)]
        xc = [sb("p4_x0", [128, E], F32), sb("p4_x1", [128, E], F32)]
        scm = sb("p4_scm", [128, E], BF16)
        isb = sb("p4_isb", [128, E], F32)
        attn = [sb("p4_at0", [128, E], F32), sb("p4_at1", [128, E], F32)]
        rmst = sb("p4_rmst", [128, E], BF16)
        rmsf = sb("p4_rmsf", [128, 16, 128], BF16)
        x2 = sb("p4_x2", [128, E], F32)
        xl2 = sb("p4_xl2", [128, E], BF16)
        stg2 = sb("p4_stg2", [128, 16, 128], BF16)
        stats = sb("p4_stats", [128, 24], F32)
        mv = sb("p4_mv", [128, 2], F32)
        srt = sb("p4_srt", [128, 1], F32)
        rstd = sb("p4_rstd", [128, 1], F32)
        nmb = sb("p4_nmb", [128, 1], F32)
        msq = sb("p4_msq", [128, 1], F32)
        psA = ps("p4_psA", [128, 1024], F32)
        psB = ps("p4_psB", [128, 1024], F32)
        psT = ps("p4_psT", [128, 2048], BF16)
        psW = [ps("p4_psW0", [128, 512], F32), ps("p4_psW1", [128, 512], F32)]
        c_ld = [K.ctr("p4_ld0"), K.ctr("p4_ld1"), K.ctr("p4_ld2")]
        c_lx = [K.ctr("p4_lx0"), K.ctr("p4_lx1")]
        c_stx = K.ctr("p4_stx")
        c_sts = K.ctr("p4_sts")
        c_wo = K.ctr("p4_wo")
        c_wo2 = K.ctr("p4_wo2")
        K.dma_batch("sync", c_wo2, [
            lambda e: e.dma_start(out=t_dmT.ap().rearrange("s (h c) -> s h c", h=NH), in_=d_dmT.rearrange("h s c -> s h c")),
            lambda e: e.dma_start(out=t_rdec[:], in_=d_rdec[:, :]),
            lambda e: e.dma_start(out=t_rmsw[:], in_=d_rmsw[:, :]),
        ])
        Wor = d_wo.rearrange("(c p) n -> p c n", p=128)
        K.dma_batch("gpsimd", c_wo, [
            (lambda e, k4=k4: e.dma_start(out=wob[:, 4 * k4:4 * (k4 + 1), :],
                                          in_=Wor[:, 4 * k4:4 * (k4 + 1), :]))
            for k4 in range(4)])
        def f_scalew(e):
            last = None
            for cc in range(16):
                last = e.tensor_scalar_mul(wob[:, cc, :], wob[:, cc, :], t_rmsw[:, cc:cc + 1])
            return last
        tok_wo = K.emit("vector", f_scalew, wait=[c_wo.tok(), c_wo2.tok()], inc=[(K.dve, 1)], rd=[wob], wr=[wob])

        scan_t, tail_t = {}, {}

        def emit_scan(c):
            p = c % 3
            px = c % 2
            pa = c % 2
            T = {}
            w = []
            if c - 3 in scan_t:
                w.append((K.pe, scan_t[c - 3]["kvu1"]))
            K.dma_batch("sync", c_ld[p], [
                lambda e, c=c, p=p: e.dma_start(out=qc[p][:], in_=d_qfm[c, :, :]),
                lambda e, c=c, p=p: e.dma_start(out=kc[p][:], in_=d_kfm[c, :, :]),
                lambda e, c=c, p=p: e.dma_start(out=kpc[p][:], in_=d_kp[c, :, :]),
                lambda e, c=c, p=p: e.dma_start(out=vc[p][:], in_=d_vs[c, :, :]),
            ], wait=w)
            w = []
            if c - 2 in tail_t:
                w.append((K.dve, tail_t[c - 2]["woe3"]))
            K.dma_batch("sync", c_lx[px], [
                lambda e, c=c, px=px: e.dma_start(out=xc[px][:], in_=d_x[c * 128:(c + 1) * 128, :]),
            ], wait=w)
            ldv = c_ld[p].tok()
            for hb in range(2):
                h0 = hb * 8
                def f_sc(e, p=p, h0=h0):
                    last = None
                    for j in range(8):
                        h = h0 + j
                        last = e.matmul(psA[:, j * 128:(j + 1) * 128], kc[p][:, h * 128:(h + 1) * 128],
                                        qc[p][:, h * 128:(h + 1) * 128], start=(j % 4 == 0), stop=(j % 4 == 3),
                                        skip_group_check=True)
                    return last
                w = [ldv]
                if hb == 1:
                    w.append((K.dve, T["cmb0"]))
                elif c - 1 in scan_t:
                    w.append((K.dve, scan_t[c - 1]["cmb1"]))
                if c - 1 in scan_t:
                    w.append((K.dve, scan_t[c - 1]["upd"]))
                t_sc = K.emit("tensor", f_sc, wait=w, inc=[(K.pe, 1)])
                t_mask = K.emit("vector",
                    lambda e, h0=h0: e.tensor_mul(scm[:, h0 * 128:(h0 + 8) * 128], psA[:],
                                                  t_dmT[:, h0 * 128:(h0 + 8) * 128]),
                    wait=[(K.pe, t_sc), c_wo2.tok()], inc=[(K.dve, 1)], rd=[], wr=[scm])
                def f_ii(e, p=p, h0=h0):
                    last = None
                    for j in range(8):
                        h = h0 + j
                        last = e.matmul(psA[:, j * 128:(j + 1) * 128], scm[:, h * 128:(h + 1) * 128],
                                        vc[p][:, h * 128:(h + 1) * 128], start=(j % 4 == 0), stop=(j % 4 == 3),
                                        skip_group_check=True)
                    for j in range(8):
                        h = h0 + j
                        last = e.matmul(psB[:, j * 128:(j + 1) * 128], qc[p][:, h * 128:(h + 1) * 128],
                                        t_kvbf[:, h * 128:(h + 1) * 128], start=(j % 4 == 0), stop=(j % 4 == 3),
                                        skip_group_check=True)
                    return last
                w = [(K.dve, t_mask)]
                if hb == 1:
                    w.append((K.dve, T["upd0"]))
                elif c - 1 in scan_t:
                    w.append((K.dve, scan_t[c - 1]["upd"]))
                t_ii = K.emit("tensor", f_ii, wait=w, inc=[(K.pe, 1)])
                def f_iev(e, h0=h0):
                    last = None
                    for j in range(8):
                        h = h0 + j
                        last = e.activation(isb[:, h * 128:(h + 1) * 128], psB[:, j * 128:(j + 1) * 128],
                                            AF.Identity, scale=t_rdec[:, h:h + 1])
                    return last
                t_iev = K.emit("scalar", f_iev, wait=[(K.pe, t_ii)], inc=[(K.act, 1)])
                w = [(K.act, t_iev)]
                if c - 2 in tail_t:
                    w.append((K.act, tail_t[c - 2]["rap"]))
                t_cmb = K.emit("vector",
                    lambda e, pa=pa, h0=h0: e.tensor_add(attn[pa][:, h0 * 128:(h0 + 8) * 128], psA[:],
                                                         isb[:, h0 * 128:(h0 + 8) * 128]),
                    wait=w, inc=[(K.dve, 1)], rd=[isb], wr=[attn[pa]])
                def f_kvu(e, p=p, h0=h0):
                    last = None
                    for j in range(8):
                        h = h0 + j
                        last = e.matmul(psB[:, j * 128:(j + 1) * 128], kpc[p][:, h * 128:(h + 1) * 128],
                                        vc[p][:, h * 128:(h + 1) * 128], start=(j % 4 == 0), stop=(j % 4 == 3),
                                        skip_group_check=True)
                    return last
                t_kvu = K.emit("tensor", f_kvu, wait=[(K.act, t_iev)], inc=[(K.pe, 1)])
                vh = K.dve.reserve(1) if hb == 1 else None
                def f_upd(e, h0=h0, hb=hb, vh=vh):
                    last = None
                    for j in range(8):
                        h = h0 + j
                        last = e.scalar_tensor_tensor(
                            t_kv[:, h * 128:(h + 1) * 128], t_kv[:, h * 128:(h + 1) * 128],
                            rC[h], psB[:, j * 128:(j + 1) * 128], ALU.mult, ALU.add)
                    if hb == 1:
                        last.then_inc(K.dve.h, 1)
                        e.wait_ge(K.dve.h, vh)
                        last = e.tensor_copy(t_kvbf[:], t_kv[:])
                    return last
                t_upd = K.emit("vector", f_upd, wait=[(K.pe, t_kvu)], inc=[(K.dve, 1)],
                               rd=[t_kv], wr=[t_kv, t_kvbf])
                if hb == 0:
                    T["cmb0"], T["upd0"] = t_cmb, t_upd
                else:
                    T["cmb1"], T["upd"] = t_cmb, t_upd
                    T["kvu1"] = t_kvu
            scan_t[c] = T

        def emit_tail(c):
            p = c % 2
            T = {}
            v1, v2, v3 = K.dve.reserve(1), K.dve.reserve(1), K.dve.reserve(1)
            def f_rstats(e, p=p, v1=v1, v2=v2, v3=v3):
                st = None
                for c4 in range(4):
                    st = e.bn_stats(stats[:, 6 * c4:6 * c4 + 6], attn[p][:, 512 * c4:512 * (c4 + 1)])
                st.then_inc(K.dve.h, 1)
                e.wait_ge(K.dve.h, v1)
                e.bn_aggr(mv[:], stats[:, 0:24]).then_inc(K.dve.h, 1)
                e.wait_ge(K.dve.h, v2)
                e.tensor_mul(msq[:], mv[:, 0:1], mv[:, 0:1]).then_inc(K.dve.h, 1)
                e.wait_ge(K.dve.h, v3)
                return e.tensor_add(msq[:], msq[:], mv[:, 1:2])
            w = [(K.dve, scan_t[c]["cmb1"])]
            if c - 1 in tail_t:
                w.append((K.act, tail_t[c - 1]["ln2ap"]))
            t_rs = K.emit("vector", f_rstats, wait=w, inc=[(K.dve, 1)],
                          rd=[attn[p]], wr=[stats, mv, msq])
            va = K.act.reserve(1)
            def f_rsq(e, va=va):
                e.activation(srt[:], msq[:], AF.Ln, bias=t_eps5[:]).then_inc(K.act.h, 1)
                e.wait_ge(K.act.h, va)
                return e.activation(rstd[:], srt[:], AF.Exp, scale=t_nhalf[:, 0:1])
            t_sq = K.emit("scalar", f_rsq, wait=[(K.dve, t_rs)], inc=[(K.act, 1)])
            w = [(K.act, t_sq)]
            if c - 1 in tail_t:
                w.append((K.pe, tail_t[c - 1]["tr1"]))
            T["rap"] = K.emit("scalar", lambda e, p=p: e.activation(rmst[:], attn[p][:], AF.Identity, scale=rstd[:]),
                              wait=w, inc=[(K.act, 1)])
            def f_tr1(e):
                last = None
                for cc in range(16):
                    last = e.matmul(psT[:, cc * 128:(cc + 1) * 128], rmst[:, cc * 128:(cc + 1) * 128],
                                    t_idbf[:], is_transpose=True, start=(cc % 8 == 0), stop=(cc % 8 == 7),
                                    skip_group_check=True)
                return last
            w = [(K.act, T["rap"])]
            if c - 1 in tail_t:
                w.append((K.dve, tail_t[c - 1]["ev2"]))
            T["tr1"] = K.emit("tensor", f_tr1, wait=w, inc=[(K.pe, 1)])
            t_ev1 = K.emit("vector", lambda e: e.tensor_copy(rmsf[:], psT.ap().rearrange("p (c t) -> p c t", c=16)),
                           wait=[(K.pe, T["tr1"])], inc=[(K.dve, 1)], rd=[], wr=[rmsf])
            t_woe = {}
            for half in range(2):
                nb0, nb1 = 2 * half, 2 * half + 1
                def f_wo(e, nb0=nb0, nb1=nb1):
                    last = None
                    for cc in range(16):
                        e.matmul(psW[0][:], rmsf[:, cc, :], wob[:, cc, nb0 * 512:(nb0 + 1) * 512],
                                 start=(cc == 0), stop=(cc == 15))
                        last = e.matmul(psW[1][:], rmsf[:, cc, :], wob[:, cc, nb1 * 512:(nb1 + 1) * 512],
                                        start=(cc == 0), stop=(cc == 15))
                    return last
                w = [(K.dve, t_ev1), (K.dve, tok_wo)]
                if half == 1:
                    w.append((K.dve, t_woe[1]))
                elif c - 1 in tail_t:
                    w.append((K.dve, tail_t[c - 1]["woe3"]))
                t_mmwo = K.emit("tensor", f_wo, wait=w, inc=[(K.pe, 1)])
                w = [(K.pe, t_mmwo), c_lx[c % 2].tok()]
                if c - 1 in tail_t and half == 0:
                    w.append(c_stx.tok())
                    w.append((K.act, tail_t[c - 1]["ln2ap"]))
                def f_add(e, nb0=nb0, nb1=nb1, p=p):
                    e.tensor_add(x2[:, nb0 * 512:(nb0 + 1) * 512], xc[p][:, nb0 * 512:(nb0 + 1) * 512], psW[0][:])
                    return e.tensor_add(x2[:, nb1 * 512:(nb1 + 1) * 512], xc[p][:, nb1 * 512:(nb1 + 1) * 512], psW[1][:])
                t_woe[nb1] = t_woe[nb0] = K.emit("vector", f_add, wait=w, inc=[(K.dve, 1)],
                                                 rd=[xc[p]], wr=[x2])
            T["woe3"] = t_woe[3]
            K.emit("sync", lambda e, c=c: e.dma_start(out=d_out[c * 128:(c + 1) * 128, :], in_=x2[:]),
                   wait=[(K.dve, t_woe[3])], inc=[(c_stx, 16)])
            v1, v2 = K.dve.reserve(1), K.dve.reserve(1)
            def f_l2s(e, v1=v1, v2=v2):
                st = None
                for c4 in range(4):
                    st = e.bn_stats(stats[:, 6 * c4:6 * c4 + 6], x2[:, 512 * c4:512 * (c4 + 1)])
                st.then_inc(K.dve.h, 1)
                e.wait_ge(K.dve.h, v1)
                e.bn_aggr(mv[:], stats[:, 0:24]).then_inc(K.dve.h, 1)
                e.wait_ge(K.dve.h, v2)
                return e.tensor_scalar_mul(nmb[:], mv[:, 0:1], -1.0)
            t_l2s = K.emit("vector", f_l2s, wait=[(K.dve, t_woe[3]), (K.act, t_sq)], inc=[(K.dve, 1)],
                           rd=[x2], wr=[stats, mv, nmb])
            va = K.act.reserve(1)
            def f_l2q(e, va=va):
                e.activation(srt[:], mv[:, 1:2], AF.Ln, bias=t_eps5[:]).then_inc(K.act.h, 1)
                e.wait_ge(K.act.h, va)
                return e.activation(rstd[:], srt[:], AF.Exp, scale=t_nhalf[:, 0:1])
            t_l2q = K.emit("scalar", f_l2q, wait=[(K.dve, t_l2s)], inc=[(K.act, 1)])
            t_l2r = K.emit("vector", lambda e: e.tensor_mul(nmb[:], nmb[:], rstd[:]),
                           wait=[(K.act, t_l2q)], inc=[(K.dve, 1)], rd=[rstd, nmb], wr=[nmb])
            w = [(K.dve, t_l2r)]
            if c - 1 in tail_t:
                w.append((K.pe, tail_t[c - 1]["tr2"]))
            T["ln2ap"] = K.emit("scalar",
                lambda e: e.activation(xl2[:], x2[:], AF.Identity, scale=rstd[:], bias=nmb[:]),
                wait=w, inc=[(K.act, 1)])
            def f_tr2(e):
                last = None
                for cc in range(16):
                    last = e.matmul(psT[:, cc * 128:(cc + 1) * 128], xl2[:, cc * 128:(cc + 1) * 128],
                                    t_idbf[:], is_transpose=True, start=(cc % 8 == 0), stop=(cc % 8 == 7),
                                    skip_group_check=True)
                return last
            T["tr2"] = K.emit("tensor", f_tr2, wait=[(K.act, T["ln2ap"]), (K.dve, t_ev1)], inc=[(K.pe, 1)])
            w = [(K.pe, T["tr2"]), c_sts.tok()]
            T["ev2"] = K.emit("vector", lambda e: e.tensor_copy(stg2[:], psT.ap().rearrange("p (c t) -> p c t", c=16)),
                              wait=w, inc=[(K.dve, 1)], rd=[], wr=[stg2])
            K.emit("sync", lambda e, c=c: e.dma_start(
                out=d_x2T[:, :, c * 128:(c + 1) * 128].rearrange("k f t -> f k t"), in_=stg2[:]),
                wait=[(K.dve, T["ev2"])], inc=[(c_sts, 16)])
            T["done"] = T["ev2"]
            tail_t[c] = T

        for c in range(G):
            emit_scan(c)
            if c > 0:
                emit_tail(c - 1)
        emit_tail(G - 1)
        K.emit("sync", None, wait=[c_stx.tok(), c_sts.tok()])
        K.barrier()

    def p5():
        nc.sbuf_base = base_sbuf
        nc.psum_base = base_psum
        NTB = G // 4
        xpn = sb("p5_xpn", [128, 16, Tc], BF16)
        wg = sb("p5_wg", [128, 16, 256], BF16)
        wqb = [sb(f"p5_wq{i}", [128, 16, 256], BF16) for i in range(3)]
        wabs = sb("p5_wabs", [128, 4096], BF16)
        sgn = sb("p5_sgn", [128, 4096], BF16)
        srep = sb("p5_srep", [128, 16, 256], BF16)
        sbf = [sb(f"p5_sb{i}", [2, 512], BF16) for i in range(4)]
        sil = [sb("p5_si0", [128, 512], BF16), sb("p5_si1", [128, 512], BF16)]
        hst = [sb("p5_h0", [128, 512], BF16), sb("p5_h1", [128, 512], BF16)]
        wpin = sb("p5_wpin", [128, E], BF16)
        wpo = sb("p5_wpo", [128, E], BF16)
        qg = sb("p5_qg", [128, E], BF16)
        qsb = [sb("p5_qsb0", [2, 512], BF16), sb("p5_qsb1", [2, 512], BF16)]
        psg = [ps("p5_psg0", [128, 512], F32), ps("p5_psg1", [128, 512], F32)]
        psu = [ps("p5_psu0", [128, 512], F32), ps("p5_psu1", [128, 512], F32)]
        pss = [ps("p5_pss0", [128, 512], F32), ps("p5_pss1", [128, 512], F32)]
        psq = [ps("p5_psq0", [128, 512], F32), ps("p5_psq1", [128, 512], F32)]
        c_pn = K.ctr("p5_pn")
        c_ldg = K.ctr("p5_ldg")
        c_sth = [K.ctr("p5_sth0"), K.ctr("p5_sth1")]
        c_ldq = [K.ctr("p5_ldq0"), K.ctr("p5_ldq1")]
        c_stq = K.ctr("p5_stq0")
        pnq = []
        for q in range(4):
            K.dma_batch("sync", c_pn,
                        [(lambda e, kc=kc, q=q: e.dma_start(
                            out=xpn[:, kc, q * 1024:(q + 1) * 1024],
                            in_=d_x2T[kc, :, q * 1024:(q + 1) * 1024]))
                         for kc in range(16)])
            pnq.append(c_pn.tok())
        pnv = pnq[-1]
        Wgr = d_wguT.rearrange("(kc p) n -> p kc n", p=128)
        quant_t, wpq_t, gu_h = {}, {}, {}

        def quant_start(m):
            p = m % 3
            T = {"m": m}
            w = []
            if m - 1 in quant_t:
                w.append((K.act, quant_t[m - 1]["sgn"]))
            K.dma_batch("gpsimd", c_ldg, [
                (lambda e, m=m, k8=k8: e.dma_start(
                    out=wg[:, 8 * k8:8 * (k8 + 1), 0:128],
                    in_=Wgr[:, 8 * k8:8 * (k8 + 1), m * 128:(m + 1) * 128]))
                for k8 in range(2)] + [
                (lambda e, m=m, k8=k8: e.dma_start(
                    out=wg[:, 8 * k8:8 * (k8 + 1), 128:256],
                    in_=Wgr[:, 8 * k8:8 * (k8 + 1), HFF + m * 128:HFF + (m + 1) * 128]))
                for k8 in range(2)], wait=w)
            ldv = c_ldg.tok()
            w = [ldv]
            if m - 1 in quant_t:
                w.append((K.dve, quant_t[m - 1]["tt"]))
            wgf = wg.ap().rearrange("p k c -> p (k c)")
            T["abs"] = K.emit("scalar", lambda e, wgf=wgf: e.activation(wabs[:], wgf, AF.Abs, scale=t_two[:, 0:1]),
                              wait=w, inc=[(K.act, 1)])
            T["sgn"] = K.emit("scalar", lambda e, wgf=wgf: e.activation(sgn[:], wgf, AF.Sign),
                              wait=[], inc=[(K.act, 1)])
            T["ev"] = {}
            quant_t[m] = T
            return T

        ring = [pss[0], pss[1], psq[0], psq[1]]
        rotc = [0]
        rg = {}

        def quant_slice_a(T, kk2):
            bA, bB = rotc[0] % 4, (rotc[0] + 1) % 4
            rotc[0] += 2
            T["bk", kk2] = (bA, bB)
            wk = [(K.act, T["abs"])]
            if bA in rg:
                wk.append(rg[bA])
            t_smm = K.emit("tensor",
                lambda e, kk2=kk2, bA=bA: e.matmul(ring[bA][0:2, 0:512], t_E2[:],
                                                   wabs[:, kk2 * 512:(kk2 + 1) * 512], start=True, stop=True,
                                                   skip_group_check=True),
                wait=wk, inc=[(K.pe, 1)])
            rp = kk2 % 4
            wc = [(K.pe, t_smm)]
            if ("bmm", kk2 - 4) in T:
                wc.append((K.pe, T["bmm", kk2 - 4]))
            T["clip", kk2] = K.emit("vector",
                lambda e, rp=rp, bA=bA: e.tensor_scalar_max(sbf[rp][:], ring[bA][0:2, 0:512], 1e-8),
                wait=wc, inc=[(K.dve, 1)], rd=[], wr=[sbf[rp]])
            rg[bA] = (K.dve, T["clip", kk2])

        def quant_slice_b(T, kk2):
            bA, bB = T["bk", kk2]
            rp = kk2 % 4
            wb = [(K.dve, T["clip", kk2])]
            if bB in rg:
                wb.append(rg[bB])
            t_bmm = K.emit("tensor",
                lambda e, rp=rp, bB=bB: e.matmul(ring[bB][:, 0:512], t_B2[:], sbf[rp][:], start=True, stop=True,
                                                 skip_group_check=True),
                wait=wb, inc=[(K.pe, 1)])
            T["bmm", kk2] = t_bmm
            T["ev"][kk2] = K.emit("scalar",
                lambda e, kk2=kk2, bB=bB: e.activation(srep[:, 2 * kk2:2 * kk2 + 2, :],
                                                       ring[bB].ap().rearrange("p (a b) -> p a b", a=2),
                                                       AF.Identity),
                wait=[(K.pe, t_bmm)], inc=[(K.act, 1)])
            rg[bB] = (K.act, T["ev"][kk2])

        def quant_tail(T):
            m = T["m"]
            p = m % 3
            srf = srep.ap().rearrange("p k c -> p (k c)")
            wqf = wqb[p].ap().rearrange("p k c -> p (k c)")
            w = [(K.act, T["ev"][7]), (K.act, T["sgn"])]
            if m - 3 in quant_t:
                w.append((K.pe, quant_t[m - 3]["lastmm"]))
            K.emit("vector", lambda e, wqf=wqf, srf=srf: e.tensor_tensor(wqf, wabs[:], srf, ALU.is_gt),
                   wait=w, inc=[(K.dve, 1)], rd=[wabs, srep], wr=[wqb[p]])
            K.emit("vector", lambda e, wqf=wqf, srf=srf: e.tensor_tensor(wqf, wqf, srf, ALU.mult),
                   wait=[], inc=[(K.dve, 1)], rd=[wqb[p], srep], wr=[wqb[p]])
            T["tt"] = K.emit("vector", lambda e, wqf=wqf: e.tensor_tensor(wqf, wqf, sgn[:], ALU.mult),
                             wait=[], inc=[(K.dve, 1)], rd=[wqb[p], sgn], wr=[wqb[p]])

        def wpq_start(j):
            p = j % 2
            T = {"j": j, "ev": {}}
            w = []
            if j - 1 in wpq_t:
                w.append((K.act, wpq_t[j - 1]["sgn"]))
                w.append((K.dve, wpq_t[j - 1]["tt"]))
            K.emit("gpsimd", lambda e, j=j: e.dma_start(out=wpin[:], in_=d_wpT[j * 128:(j + 1) * 128, :]),
                   wait=w, inc=[(c_ldq[p], 16)])
            w = [c_ldq[p].tok()]
            if j - 1 in wpq_t:
                w.append((K.dve, wpq_t[j - 1]["tt"]))
            T["abs"] = K.emit("scalar", lambda e: e.activation(wpo[:], wpin[:], AF.Abs, scale=t_two[:, 0:1]),
                              wait=w + [c_stq.tok()], inc=[(K.act, 1)])
            T["sgn"] = K.emit("scalar", lambda e: e.activation(qg[:], wpin[:], AF.Sign),
                              wait=[], inc=[(K.act, 1)])
            wpq_t[j] = T
            return T

        def wpq_slice_a(T, nb):
            bA, bB = rotc[0] % 4, (rotc[0] + 1) % 4
            rotc[0] += 2
            T["bk", nb] = (bA, bB)
            wk = [(K.act, T["abs"])]
            if bA in rg:
                wk.append(rg[bA])
            t_smm = K.emit("tensor",
                lambda e, nb=nb, bA=bA: e.matmul(ring[bA][0:2, :], t_E2[:],
                                                 wpo[:, nb * 512:(nb + 1) * 512], start=True, stop=True,
                                                 skip_group_check=True),
                wait=wk, inc=[(K.pe, 1)])
            rp = nb % 2
            wc = [(K.pe, t_smm)]
            if ("bmm", nb - 2) in T:
                wc.append((K.pe, T["bmm", nb - 2]))
            T["clip", nb] = K.emit("vector",
                lambda e, rp=rp, bA=bA: e.tensor_scalar_max(qsb[rp][:], ring[bA][0:2, :], 1e-8),
                wait=wc, inc=[(K.dve, 1)], rd=[], wr=[qsb[rp]])
            rg[bA] = (K.dve, T["clip", nb])

        def wpq_slice_b(T, nb):
            bA, bB = T["bk", nb]
            rp = nb % 2
            wb = [(K.dve, T["clip", nb])]
            if bB in rg:
                wb.append(rg[bB])
            t_bmm = K.emit("tensor",
                lambda e, rp=rp, bB=bB: e.matmul(ring[bB][:, :], t_B2[:], qsb[rp][:], start=True, stop=True,
                                                 skip_group_check=True),
                wait=wb, inc=[(K.pe, 1)])
            T["bmm", nb] = t_bmm
            T["ev"][nb] = K.emit("scalar",
                lambda e, nb=nb, bB=bB: e.activation(wpin[:, nb * 512:(nb + 1) * 512], ring[bB][:, :], AF.Identity),
                wait=[(K.pe, t_bmm)], inc=[(K.act, 1)])
            rg[bB] = (K.act, T["ev"][nb])

        def wpq_tail(T):
            j = T["j"]
            w = [(K.act, T["ev"][3]), (K.act, T["sgn"])]
            K.emit("vector", lambda e: e.tensor_tensor(wpo[:], wpo[:], wpin[:], ALU.is_gt),
                   wait=w, inc=[(K.dve, 1)], rd=[wpo, wpin], wr=[wpo])
            K.emit("vector", lambda e: e.tensor_tensor(wpo[:], wpo[:], wpin[:], ALU.mult),
                   wait=[], inc=[(K.dve, 1)], rd=[wpo, wpin], wr=[wpo])
            T["tt"] = K.emit("vector", lambda e: e.tensor_tensor(wpo[:], wpo[:], qg[:], ALU.mult),
                             wait=[], inc=[(K.dve, 1)], rd=[wpo, qg], wr=[wpo])
            K.emit("sync", lambda e, j=j: e.dma_start(out=d_wpq[j, :, :], in_=wpo[:]),
                   wait=[(K.dve, T["tt"])], inc=[(c_stq, 16)])

        def emit_gu_tb(m, tb):
            p = m % 3
            T = quant_t[m]
            pb = tb % 2
            def f_g(e, p=p, tb=tb, pb=pb):
                last = None
                for kcx in range(16):
                    last = e.matmul(psg[pb][:], wqb[p][:, kcx, 0:128], xpn[:, kcx, tb * 512:(tb + 1) * 512],
                                    start=(kcx == 0), stop=(kcx == 15))
                for kcx in range(16):
                    last = e.matmul(psu[pb][:], wqb[p][:, kcx, 128:256], xpn[:, kcx, tb * 512:(tb + 1) * 512],
                                    start=(kcx == 0), stop=(kcx == 15))
                return last
            w = [(K.dve, T["tt"]), pnq[tb // 2]]
            t_h = gu_h.setdefault(m, {})
            if tb >= 2:
                w.append((K.dve, t_h[tb - 2]))
            elif m - 1 in gu_h:
                w.append((K.dve, gu_h[m - 1][max(0, NTB - 2 + tb)]))
            t_mm = K.emit("tensor", f_g, wait=w, inc=[(K.pe, 1)])
            t_sil = K.emit("scalar", lambda e, pb=pb: e.activation(sil[pb][:], psg[pb][:], AF.Silu),
                           wait=[(K.pe, t_mm)], inc=[(K.act, 1)])
            t_h[tb] = K.emit("vector", lambda e, pb=pb: e.tensor_mul(hst[pb][:], sil[pb][:], psu[pb][:]),
                             wait=[(K.act, t_sil), c_sth[pb].tok()], inc=[(K.dve, 1)],
                             rd=[sil[pb]], wr=[hst[pb]])
            K.emit("sync", lambda e, m=m, tb=tb, pb=pb: e.dma_start(
                out=d_hs[m, :, tb * 512:(tb + 1) * 512], in_=hst[pb][:]),
                wait=[(K.dve, t_h[tb])], inc=[(c_sth[pb], 16)])
            quant_t[m]["lastmm"] = t_mm

        def ab_units(T, n, fa, fb):
            units = [lambda: fa(T, 0)]
            for kk in range(1, n):
                units.append(lambda kk=kk: fa(T, kk))
                units.append(lambda kk=kk - 1: fb(T, kk))
            units.append(lambda: fb(T, n - 1))
            return units

        for mm0 in (0, 1):
            T0 = quant_start(mm0)
            for u in ab_units(T0, 8, quant_slice_a, quant_slice_b):
                u()
            quant_tail(T0)
        for m in range(NPAIR):
            units = []
            if m + 2 < NPAIR:
                Tn = quant_start(m + 2)
                units += ab_units(Tn, 8, quant_slice_a, quant_slice_b)
                units.append(lambda Tn=Tn: quant_tail(Tn))
            Tw = wpq_start(m)
            units += ab_units(Tw, 4, wpq_slice_a, wpq_slice_b)
            units.append(lambda Tw=Tw: wpq_tail(Tw))
            for tb in range(NTB):
                emit_gu_tb(m, tb)
            for u in units:
                u()
        K.emit("sync", None, wait=[c_sth[0].tok(), c_sth[1].tok(), c_stq.tok()])
        K.barrier()

    def p6():
        nc.sbuf_base = base_sbuf
        nc.psum_base = base_psum
        NT6 = G // 4
        ht = sb("p6_ht", [128, 64, 512], BF16)
        h2r = [sb(f"p6_h2_{i}", [128, 4096], BF16) for i in range(2)]
        wpd = [sb(f"p6_wpd{i}", [128, 32, 512], BF16) for i in range(3)]
        stage = [sb("p6_st0", [128, 512], F32), sb("p6_st1", [128, 512], F32)]
        srow = sb("p6_srow", [1, 512], F32)
        srow2 = sb("p6_srow2", [1, 512], F32)
        rstd6 = sb("p6_rstd", [128, 4], F32)
        psr = ps("p6_psr", [128, 512], F32)
        psc = ps("p6_psc", [128, 512], F32)
        psp = [ps("p6_psp0", [128, 512], F32), ps("p6_psp1", [128, 512], F32)]
        c_ldh = K.ctr("p6_ldh")
        c_ldw = [K.ctr("p6_ldw0"), K.ctr("p6_ldw1"), K.ctr("p6_ldw2")]
        c_acc = [K.ctr("p6_acc0"), K.ctr("p6_acc1")]
        proj_done = {}
        slot_reader = {}
        prev_ev6 = None
        acc_tok = {}
        wslot = 0
        prev_p1last = prev_pmlast = None
        for tt in range(NT6):
            w = [prev_p1last] if prev_p1last is not None else []
            K.dma_batch("sync", c_ldh, [
                (lambda e, tt=tt, j8=j8: e.dma_start(
                    out=ht[:, j8 * 8:(j8 + 1) * 8, :],
                    in_=d_hs[j8 * 8:(j8 + 1) * 8, :, tt * 512:(tt + 1) * 512].rearrange("c p t -> p c t")))
                for j8 in range(4)], wait=w)
            ldhvA = c_ldh.tok()
            w = [prev_pmlast] if prev_pmlast is not None else []
            K.dma_batch("sync", c_ldh, [
                (lambda e, tt=tt, j8=j8: e.dma_start(
                    out=ht[:, j8 * 8:(j8 + 1) * 8, :],
                    in_=d_hs[j8 * 8:(j8 + 1) * 8, :, tt * 512:(tt + 1) * 512].rearrange("c p t -> p c t")))
                for j8 in range(4, 8)], wait=w)
            ldhv = c_ldh.tok()
            mm_toks = {}
            ssq_units = []
            def u_h2(hb8):
                rp = hb8 % 2
                w = [ldhvA if hb8 < 4 else ldhv]
                if hb8 - 2 in mm_toks:
                    w.append((K.pe, mm_toks[hb8 - 2]))
                T6[hb8] = K.emit("vector",
                    lambda e, hb8=hb8, rp=rp: e.tensor_mul(
                        h2r[rp].ap().rearrange("p (c t) -> p c t", c=8),
                        ht[:, hb8 * 8:(hb8 + 1) * 8, :],
                        ht[:, hb8 * 8:(hb8 + 1) * 8, :]),
                    wait=w, inc=[(K.dve, 1)], rd=[ht], wr=[h2r[rp]])
            def u_ssq(hb8):
                rp = hb8 % 2
                def f_ssq(e, hb8=hb8, rp=rp):
                    last = None
                    for jj in range(8):
                        last = e.matmul(psr[0:1, :], t_onebf[:],
                                        h2r[rp][:, jj * 512:(jj + 1) * 512],
                                        start=(hb8 == 0 and jj == 0), stop=(hb8 == 7 and jj == 7))
                    return last
                mm_toks[hb8] = K.emit("tensor", f_ssq, wait=[(K.dve, T6[hb8])], inc=[(K.pe, 1)])
            def u_rstd():
                va = K.act.reserve(1)
                def f_sq(e, va=va):
                    e.activation(srow[:], psr[0:1, :], AF.Ln, scale=t_i8k[:], bias=t_epsf[:]).then_inc(K.act.h, 1)
                    e.wait_ge(K.act.h, va)
                    return e.activation(srow2[:], srow[:], AF.Exp, scale=t_nh1[:, 0:1])
                t_sq = K.emit("scalar", f_sq, wait=[(K.pe, mm_toks[7])], inc=[(K.act, 1)])
                def f_cols(e):
                    last = None
                    for j in range(4):
                        last = e.matmul(psc[:, j:j + 1], srow2[0:1, j * 128:(j + 1) * 128], t_onef[:],
                                        start=(j == 0), stop=(j == 3), skip_group_check=True)
                    return last
                t_cols = K.emit("tensor", f_cols, wait=[(K.act, t_sq)], inc=[(K.pe, 1)])
                tr = K.emit("vector", lambda e: e.tensor_copy(rstd6[:], psc[:, 0:4]),
                            wait=[(K.pe, t_cols)], inc=[(K.dve, 1)], rd=[], wr=[rstd6])
                t_revs.append(tr)
            T6 = {}
            t_revs = []
            for hb8 in range(8):
                ssq_units.append(lambda hb8=hb8: u_h2(hb8))
                ssq_units.append(lambda hb8=hb8: u_ssq(hb8))
            ssq_units.append(u_rstd)
            sui = [0]
            def drain_unit(n=1):
                for _ in range(n):
                    if sui[0] < len(ssq_units):
                        ssq_units[sui[0]]()
                        sui[0] += 1
            for d in range(4):
                for half in range(2):
                    slot = wslot % 3
                    w = [slot_reader[slot]] if slot in slot_reader else []
                    K.dma_batch("sync", c_ldw[slot], [
                        (lambda e, d=d, half=half, slot=slot, j4=j4: e.dma_start(
                            out=wpd[slot][:, j4 * 8:(j4 + 1) * 8, :],
                            in_=d_wpq[half * 32 + j4 * 8:half * 32 + (j4 + 1) * 8, :, d * 512:(d + 1) * 512]
                            .rearrange("c p n -> p c n")))
                        for j4 in range(4)], wait=w)
                    wslot += 1
                sA, sB = (wslot - 2) % 3, (wslot - 1) % 3
                ev6 = {}
                t_pm = None
                for t1 in range(4):
                    drain_unit(3)
                    pb = t1 % 2
                    def f_p1(e, t1=t1, pb=pb, sA=sA):
                        last = None
                        for hf in range(32):
                            last = e.matmul(psp[pb][:], ht[:, hf, t1 * 128:(t1 + 1) * 128],
                                            wpd[sA][:, hf, :], start=(hf == 0), stop=False)
                        return last
                    def f_p2(e, t1=t1, pb=pb, sB=sB):
                        last = None
                        for hf in range(32, 64):
                            last = e.matmul(psp[pb][:], ht[:, hf, t1 * 128:(t1 + 1) * 128],
                                            wpd[sB][:, hf - 32, :], start=False, stop=(hf == 63))
                        return last
                    w = [c_ldw[sA].tok(), ldhvA]
                    if t1 >= 2:
                        w.append((K.act, ev6[t1 - 2]))
                    elif prev_ev6 is not None:
                        w.append((K.act, prev_ev6[t1 + 2]))
                    t_p1 = K.emit("tensor", f_p1, wait=w, inc=[(K.pe, 1)])
                    t_pm = K.emit("tensor", f_p2, wait=[c_ldw[sB].tok(), ldhv], inc=[(K.pe, 1)])
                    if d == 3 and t1 == 3:
                        prev_p1last = (K.pe, t_p1)
                        prev_pmlast = (K.pe, t_pm)
                    w = [(K.pe, t_pm)]
                    if t1 >= 2:
                        key = (tt, d, t1 - 2)
                    elif d > 0:
                        key = (tt, d - 1, t1 + 2)
                    else:
                        key = (tt - 1, 3, t1 + 2)
                    if key in acc_tok:
                        w.append((c_acc[pb], acc_tok[key]))
                    while sui[0] < len(ssq_units):
                        drain_unit(1)
                    w.append((K.dve, t_revs[0]))
                    ev6[t1] = K.emit("scalar",
                        lambda e, t1=t1, pb=pb: e.activation(stage[pb][:], psp[pb][:], AF.Identity,
                                                             scale=rstd6[:, t1:t1 + 1]),
                        wait=w, inc=[(K.act, 1)])
                    K.emit("gpsimd", lambda e, tt=tt, t1=t1, d=d, pb=pb: e.dma_start(
                        out=d_out[tt * 512 + t1 * 128:tt * 512 + (t1 + 1) * 128, d * 512:(d + 1) * 512],
                        in_=stage[pb][:], accum_op=ALU.add),
                        wait=[(K.act, ev6[t1])], inc=[(c_acc[pb], 16)])
                    acc_tok[(tt, d, t1)] = c_acc[pb].n
                slot_reader[sA] = (K.pe, t_pm)
                slot_reader[sB] = (K.pe, t_pm)
                prev_ev6 = ev6
            proj_done[tt] = prev_ev6[3]
        K.emit("gpsimd", None, wait=[c_acc[0].tok(), c_acc[1].tok()])
        K.barrier()

    allp = {"p0": p0, "q": lambda: p_qkv("q"), "k": lambda: p_qkv("k"),
            "v": lambda: p_qkv("v"), "px": px, "p4": p4, "p5": p5, "p6": p6}
    order = ["p0", "q", "k", "v", "px", "p4", "p5", "p6"]
    for name in order:
        if phases is None or name in phases:
            allp[name]()
    K.replay()
    return nc



def host_tables(core, G):
    b, s = core // 4, core % 4
    Tc = G * 128
    off = s * Tc
    f32 = np.float32
    r = _slopes_r()
    i128 = np.arange(128, dtype=f32)
    rdecay = np.power(r[None, :], (i128[:, None] + f32(1.0))).astype(f32)
    kd = np.power(r[:, None], (f32(127.0) - i128[None, :])).astype(f32)
    kdecay = np.repeat(kd.T[:, :, None], HD, axis=2).reshape(128, E).astype(f32)
    diff = i128[None, :] - i128[:, None]
    dmaskT = np.where(diff[None] >= 0,
                      np.power(r[:, None, None], np.where(diff >= 0, diff, 0)[None]),
                      f32(0.0)).astype(f32)
    lin = np.linspace(0.0, 1.0, HD // 4, dtype=f32)
    freq = np.power(f32(1.0 / 1024.0), lin).astype(f32)
    t = (off + np.arange(Tc, dtype=np.float64)).astype(f32)
    theta = t[:, None] * freq[None, :]
    costab = np.tile(np.cos(theta).astype(f32).reshape(G, 128, 32), (1, 1, 8))
    sintab = np.tile(np.sin(theta).astype(f32).reshape(G, 128, 32), (1, 1, 8))
    wv = np.zeros((8, NH), dtype=f32)
    for j in range(8):
        bj, sj = j // 4, j % 4
        if bj == b and sj < s:
            wv[j] = np.power(r, f32(Tc * (s - 1 - sj))).astype(f32)
    wvec = np.ascontiguousarray(np.broadcast_to(wv[None], (128, 8, NH)))
    E2t = np.zeros((128, 2), dtype=ml_dtypes.bfloat16)
    E2t[:64, 0] = ml_dtypes.bfloat16(1.0 / 128.0)
    E2t[64:, 1] = ml_dtypes.bfloat16(1.0 / 128.0)
    B2t = np.zeros((2, 128), dtype=ml_dtypes.bfloat16)
    B2t[0, :64] = ml_dtypes.bfloat16(1.0)
    B2t[1, 64:] = ml_dtypes.bfloat16(1.0)
    return dict(costab=np.ascontiguousarray(costab), sintab=np.ascontiguousarray(sintab),
                kdecay=np.ascontiguousarray(kdecay), dmaskT=np.ascontiguousarray(dmaskT),
                rdecay=np.ascontiguousarray(rdecay), wvec=wvec, E2=E2t, B2=B2t,
                id32=np.eye(128, dtype=f32), idbf=np.eye(128, dtype=ml_dtypes.bfloat16))


def make_in_maps(x, wq, wk, wv, wo, rms_w, w_gate_up, w_proj, G):
    Tc = G * 128
    wguT = np.ascontiguousarray(np.asarray(w_gate_up).T.astype(np.float32))
    wpT = np.ascontiguousarray(np.asarray(w_proj).T.astype(np.float32))
    rmsw = np.ascontiguousarray(np.asarray(rms_w).astype(np.float32).reshape(16, 128).T)
    in_maps = []
    for core in range(8):
        b, s = core // 4, core % 4
        m = {"x": np.ascontiguousarray(x[b, s * Tc:(s + 1) * Tc]),
             "wq": np.asarray(wq), "wk": np.asarray(wk), "wv": np.asarray(wv),
             "wo": np.asarray(wo), "rmsw": rmsw, "wguT": wguT, "wpT": wpT}
        m.update(host_tables(core, G))
        in_maps.append(m)
    return in_maps


def run_kernel(x, wq, wk, wv, wo, rms_w, w_gate_up, w_proj, G, dbg=False, nc=None, phases=None):
    Tc = G * 128
    T = 4 * Tc
    assert x.shape == (2, T, E)
    if nc is None:
        nc = build_nc(G, dbg, phases=phases)
    in_maps = make_in_maps(x, wq, wk, wv, wo, rms_w, w_gate_up, w_proj, G)
    res = run_bass_kernel_spmd(nc, in_maps, core_ids=list(range(8)))
    out = np.zeros((2, T, E), dtype=np.float32)
    for core in range(8):
        b, s = core // 4, core % 4
        out[b, s * Tc:(s + 1) * Tc] = res.results[core]["out"]
    return out, res



def kernel(L0_attn_x, L0_attn_Wq, L0_attn_Wk, L0_attn_Wv, L0_attn_Wo,
           L0_attn_rms_w, L0_ffn_w_gate_up, L0_ffn_w_proj):
    G = 32
    x = np.asarray(L0_attn_x)
    out, _res = run_kernel(x, np.asarray(L0_attn_Wq), np.asarray(L0_attn_Wk),
                           np.asarray(L0_attn_Wv), np.asarray(L0_attn_Wo),
                           np.asarray(L0_attn_rms_w), np.asarray(L0_ffn_w_gate_up),
                           np.asarray(L0_ffn_w_proj), G)
    return out


# version 17
# speedup vs baseline: 1.0222x; 1.0222x over previous
import math
import numpy as np
import ml_dtypes
import concourse.bass as bass
import concourse.mybir as mybir
from concourse.bass_utils import run_bass_kernel_spmd

F32 = mybir.dt.float32
BF16 = mybir.dt.bfloat16
AF = mybir.ActivationFunctionType
ALU = mybir.AluOpType

NH = 16
HD = 128
E = 2048
HFF = 8192
NPAIR = 64
ENGINES = ("tensor", "vector", "scalar", "gpsimd", "sync")


def _slopes_r():
    start = 2.0 ** (-(2.0 ** (-(math.log2(NH) - 3))))
    slopes = np.array([start * start ** i for i in range(NH)], dtype=np.float32)
    return np.exp(-slopes).astype(np.float32)


class Ctr:
    __slots__ = ("h", "n", "name")

    def __init__(self, h, name):
        self.h = h
        self.n = 0
        self.name = name

    def reserve(self, k=1):
        self.n += k
        return self.n

    def tok(self):
        return (self, self.n)


class KB:
    def __init__(self, nc):
        self.nc = nc
        self.items = []
        self.ctrs = {}
        self.last_dve_w = frozenset()
        self._dve_item_end = 0
        self.pe = self.ctr("c_pe")
        self.dve = self.ctr("c_dve")
        self.act = self.ctr("c_act")
        self.pool = self.ctr("c_pool")

    def ctr(self, name):
        if name not in self.ctrs:
            self.ctrs[name] = Ctr(self.nc.alloc_semaphore(name), name)
        return self.ctrs[name]

    def emit(self, eng, fn, wait=(), inc=(), rd=(), wr=()):
        incs = []
        tok = None
        for c, k in inc:
            tok = c.reserve(k)
            incs.append((c.h, k))
        if eng == "vector":
            rdids = {id(x) for x in rd} | {id(x) for x in wr}
            if rdids & self.last_dve_w:
                wait = list(wait) + [(self.dve, self._dve_item_end)]
            self.last_dve_w = frozenset(id(x) for x in wr)
            self._dve_item_end = self.dve.n
        self.items.append((eng, tuple((c.h, v) for c, v in wait), fn, tuple(incs), False))
        return tok

    def dma_batch(self, eng, ctr, fns, wait=()):
        ctr.reserve(16 * len(fns))
        h = ctr.h

        def f(e, fns=tuple(fns), h=h):
            last = None
            for g in fns:
                last = g(e)
                last.then_inc(h, 16)
            return last
        self.items.append((eng, tuple((c.h, v) for c, v in wait), f, (), False))
        return ctr.n

    def barrier(self):
        bar = self.ctr("c_bar")
        own = {"tensor": self.pe, "vector": self.dve, "scalar": self.act, "gpsimd": self.pool}
        for eng in ENGINES:
            w = [(own[eng], own[eng].n)] if eng in own else []
            self.emit(eng, lambda e: e.nop(), wait=w, inc=[(bar, 1)])
        target = bar.n
        for eng in ENGINES:
            self.emit(eng, None, wait=[(bar, target)])
        self.last_dve_w = frozenset()

    def replay(self):
        nc = self.nc
        items = self.items
        with nc.Block() as block:
            for engname, deco in (("tensor", block.tensor), ("vector", block.vector),
                                  ("scalar", block.scalar), ("gpsimd", block.gpsimd),
                                  ("sync", block.sync)):
                def make(engname):
                    def run(e):
                        for (eng, waits, fn, incs, hz) in items:
                            if eng != engname:
                                continue
                            for (h, v) in waits:
                                e.wait_ge(h, v)
                            inst = fn(e) if fn is not None else None
                            if incs:
                                for j, (h, k) in enumerate(incs):
                                    if j == 0:
                                        inst.then_inc(h, k)
                                    else:
                                        e.nop().then_inc(h, k)
                    return run
                deco(make(engname))


def build_nc(G, dbg=False, phases=None):
    Tc = G * 128
    nc = bass.Bass("TRN2", target_bir_lowering=False, debug=False)
    K = KB(nc)
    sb = nc.alloc_sbuf_tensor
    ps = nc.alloc_psum_tensor
    rC = [float(x) for x in np.power(_slopes_r(), np.float32(128.0)).astype(np.float32)]

    d_x = nc.dram_tensor("x", [Tc, E], F32, kind="ExternalInput").ap()
    d_wq = nc.dram_tensor("wq", [E, E], F32, kind="ExternalInput").ap()
    d_wk = nc.dram_tensor("wk", [E, E], F32, kind="ExternalInput").ap()
    d_wv = nc.dram_tensor("wv", [E, E], F32, kind="ExternalInput").ap()
    d_wo = nc.dram_tensor("wo", [E, E], F32, kind="ExternalInput").ap()
    d_rmsw = nc.dram_tensor("rmsw", [128, 16], F32, kind="ExternalInput").ap()
    d_wguT = nc.dram_tensor("wguT", [E, 2 * HFF], F32, kind="ExternalInput").ap()
    d_wpT = nc.dram_tensor("wpT", [HFF, E], F32, kind="ExternalInput").ap()
    d_cos = nc.dram_tensor("costab", [G, 128, 256], F32, kind="ExternalInput").ap()
    d_sin = nc.dram_tensor("sintab", [G, 128, 256], F32, kind="ExternalInput").ap()
    d_kdec = nc.dram_tensor("kdecay", [128, E], F32, kind="ExternalInput").ap()
    d_dmT = nc.dram_tensor("dmaskT", [16, 128, 128], F32, kind="ExternalInput").ap()
    d_rdec = nc.dram_tensor("rdecay", [128, 16], F32, kind="ExternalInput").ap()
    d_wvec = nc.dram_tensor("wvec", [128, 8, 16], F32, kind="ExternalInput").ap()
    d_E2 = nc.dram_tensor("E2", [128, 2], BF16, kind="ExternalInput").ap()
    d_B2 = nc.dram_tensor("B2", [2, 128], BF16, kind="ExternalInput").ap()
    d_id32 = nc.dram_tensor("id32", [128, 128], F32, kind="ExternalInput").ap()
    d_idbf = nc.dram_tensor("idbf", [128, 128], BF16, kind="ExternalInput").ap()

    d_out = nc.dram_tensor("out", [Tc, E], F32, kind="ExternalOutput").ap()
    sk = "ExternalOutput" if dbg else "Internal"
    d_qfm = nc.dram_tensor("qfm_s", [G, 128, E], BF16, kind=sk).ap()
    d_kfm = nc.dram_tensor("kfm_s", [G, 128, E], BF16, kind=sk).ap()
    d_kp = nc.dram_tensor("kp_s", [G, 128, E], BF16, kind=sk).ap()
    d_vs = nc.dram_tensor("v_s", [G, 128, E], BF16, kind=sk).ap()
    d_x2T = nc.dram_tensor("x2T_s", [16, 128, Tc], BF16, kind=sk).ap()
    d_hs = nc.dram_tensor("h_s", [NPAIR, 128, Tc], BF16, kind=sk).ap()
    d_wpq = nc.dram_tensor("wpq_s", [64, 128, E], BF16, kind=sk).ap()
    d_kvl = nc.dram_tensor("kvl", [128, E], F32, kind=sk).ap() if dbg else None
    d_kvin = nc.dram_tensor("kvin", [128, E], F32, kind=sk).ap() if dbg else None
    d_ccin = nc.dram_tensor("cc_in", [NH * HD, HD], BF16, kind="Internal")
    d_ccout = nc.dram_tensor("cc_out", [8 * NH * HD, HD], BF16, kind="Internal")

    t_E2 = sb("t_E2", [128, 2], BF16)
    t_B2 = sb("t_B2", [2, 128], BF16)
    t_id32 = sb("t_id32", [128, 128], F32)
    t_idbf = sb("t_idbf", [128, 128], BF16)
    t_eps5 = sb("t_eps5", [128, 1], F32)
    t_epsf = sb("t_epsf", [1, 1], F32)
    t_i8k = sb("t_i8k", [1, 1], F32)
    t_onebf = sb("t_onebf", [128, 1], BF16)
    t_onef = sb("t_onef", [1, 1], F32)
    t_two = sb("t_two", [128, 1], F32)
    t_nhalf = sb("t_nhalf", [128, 1], F32)
    t_nh1 = sb("t_nh1", [1, 1], F32)
    base_sbuf = nc.sbuf_base
    base_psum = nc.psum_base
    t_kv = sb("t_kv", [128, E], F32)
    t_kvbf = sb("t_kvbf", [128, E], BF16)
    base_kv = nc.sbuf_base

    ld0 = K.ctr("ld_const")
    K.dma_batch("sync", ld0, [
        lambda e: e.dma_start(out=t_E2[:], in_=d_E2[:, :]),
        lambda e: e.dma_start(out=t_B2[:], in_=d_B2[:, :]),
        lambda e: e.dma_start(out=t_id32[:], in_=d_id32[:, :]),
        lambda e: e.dma_start(out=t_idbf[:], in_=d_idbf[:, :]),
    ])

    def f_msets(e):
        e.memset(t_eps5[:], 1e-5)
        e.memset(t_epsf[:], float(np.finfo(np.float32).eps))
        e.memset(t_i8k[:], 1.0 / 8192.0)
        e.memset(t_onebf[:], 1.0)
        e.memset(t_onef[:], 1.0)
        e.memset(t_two[:], 2.0)
        e.memset(t_nhalf[:], -0.5)
        e.memset(t_nh1[:], -0.5)
        return e.memset(t_kv[:], 0.0)
    K.emit("vector", f_msets, inc=[(K.dve, 1)], wr=[t_kv])
    w0 = [ld0.tok(), K.dve.tok()]
    for eng in ENGINES:
        K.emit(eng, None, wait=w0)

    t_xlnT = sb("t_xlnT", [128, 16, Tc], BF16)
    epoch_sbuf = nc.sbuf_base

    def p0():
        nc.sbuf_base = epoch_sbuf
        nc.psum_base = base_psum
        xg = [sb("p0_xg0", [128, E], F32), sb("p0_xg1", [128, E], F32)]
        xl = [sb("p0_xl0", [128, E], BF16), sb("p0_xl1", [128, E], BF16)]
        stats = sb("p0_stats", [128, 24], F32)
        mv = sb("p0_mv", [128, 2], F32)
        srt = sb("p0_srt", [128, 1], F32)
        rstd = sb("p0_rstd", [128, 1], F32)
        nmb = sb("p0_nmb", [128, 1], F32)
        pt = [ps("p0_pt0", [128, 1024], BF16), ps("p0_pt1", [128, 1024], BF16)]
        c_ldx = [K.ctr("p0_ldx0"), K.ctr("p0_ldx1")]
        tok_apply, tok_tr, tok_ev = {}, {}, {}
        for g in range(G):
            p = g % 2
            w = [(K.act, tok_apply[g - 2])] if g - 2 in tok_apply else []
            K.emit("sync", lambda e, p=p, g=g: e.dma_start(out=xg[p][:], in_=d_x[g * 128:(g + 1) * 128, :]),
                   wait=w, inc=[(c_ldx[p], 16)])
            v1, v2 = K.dve.reserve(1), K.dve.reserve(1)
            def f_stats(e, p=p, v1=v1, v2=v2):
                st = None
                for c4 in range(4):
                    st = e.bn_stats(stats[:, 6 * c4:6 * c4 + 6], xg[p][:, 512 * c4:512 * (c4 + 1)])
                st.then_inc(K.dve.h, 1)
                e.wait_ge(K.dve.h, v1)
                e.bn_aggr(mv[:], stats[:, 0:24]).then_inc(K.dve.h, 1)
                e.wait_ge(K.dve.h, v2)
                return e.tensor_scalar_mul(nmb[:], mv[:, 0:1], -1.0)
            tv = K.emit("vector", f_stats, wait=[c_ldx[p].tok()], inc=[(K.dve, 1)],
                        rd=[xg[p]], wr=[nmb, mv, stats])
            va = K.act.reserve(1)
            def f_rstd(e, va=va):
                e.activation(srt[:], mv[:, 1:2], AF.Ln, bias=t_eps5[:]).then_inc(K.act.h, 1)
                e.wait_ge(K.act.h, va)
                return e.activation(rstd[:], srt[:], AF.Exp, scale=t_nhalf[:, 0:1])
            ta = K.emit("scalar", f_rstd, wait=[(K.dve, tv)], inc=[(K.act, 1)])
            tv2 = K.emit("vector", lambda e: e.tensor_mul(nmb[:], nmb[:], rstd[:]),
                         wait=[(K.act, ta)], inc=[(K.dve, 1)], rd=[rstd, nmb], wr=[nmb])
            w = [(K.dve, tv2)] + ([(K.pe, tok_tr[g - 2])] if g - 2 in tok_tr else [])
            tok_apply[g] = K.emit("scalar",
                lambda e, p=p: e.activation(xl[p][:], xg[p][:], AF.Identity, scale=rstd[:], bias=nmb[:]),
                wait=w, inc=[(K.act, 1)])
            def f_tr(e, p=p):
                last = None
                for c in range(16):
                    bank = pt[c // 8]
                    last = e.matmul(bank[:, (c % 8) * 128:(c % 8 + 1) * 128],
                                    xl[p][:, c * 128:(c + 1) * 128], t_idbf[:],
                                    is_transpose=True, start=(c % 8 == 0), stop=(c % 8 == 7),
                                    skip_group_check=True)
                return last
            w = [(K.act, tok_apply[g])] + ([(K.dve, tok_ev[g - 1])] if g - 1 in tok_ev else [])
            tok_tr[g] = K.emit("tensor", f_tr, wait=w, inc=[(K.pe, 1)])
            def f_ev(e, g=g):
                e.tensor_copy(t_xlnT[:, 0:8, g * 128:(g + 1) * 128],
                              pt[0].ap().rearrange("p (c t) -> p c t", c=8))
                return e.tensor_copy(t_xlnT[:, 8:16, g * 128:(g + 1) * 128],
                                     pt[1].ap().rearrange("p (c t) -> p c t", c=8))
            tok_ev[g] = K.emit("vector", f_ev, wait=[(K.pe, tok_tr[g])], inc=[(K.dve, 1)],
                               rd=[], wr=[t_xlnT])
        K.barrier()

    def p_qkv(which):
        nc.sbuf_base = epoch_sbuf
        nc.psum_base = base_psum
        W = {"q": d_wq, "k": d_wk, "v": d_wv}[which]
        w4 = sb(f"{which}_w4", [128, 16, 1024], BF16)
        pp = [[ps(f"{which}_ppA0", [128, 512], F32), ps(f"{which}_ppA1", [128, 512], F32)],
              [ps(f"{which}_ppB0", [128, 512], F32), ps(f"{which}_ppB1", [128, 512], F32)]]
        c_ldw = K.ctr(f"{which}_ldw")
        c_sp = [K.ctr(f"{which}_sp0"), K.ctr(f"{which}_sp1")]
        tok_mm, tok_ev, tok_rot, tok_tr, tok_st = {}, {}, {}, {}, {}
        if which in ("q", "k"):
            qtm = [sb(f"{which}_tm0", [128, 1024], F32), sb(f"{which}_tm1", [128, 1024], F32)]
            tmp = sb(f"{which}_tmp", [128, 4, 256], F32)
            cs = [sb(f"{which}_c0", [128, 256], F32), sb(f"{which}_c1", [128, 256], F32)]
            sn = [sb(f"{which}_s0", [128, 256], F32), sb(f"{which}_s1", [128, 256], F32)]
            tp = [ps(f"{which}_tp0", [128, 512], F32), ps(f"{which}_tp1", [128, 512], F32)]
            stg = [sb(f"{which}_sg0", [128, 1024], BF16), sb(f"{which}_sg1", [128, 1024], BF16)]
            c_cs = [K.ctr(f"{which}_cc0"), K.ctr(f"{which}_cc1")]
            if which == "k":
                t_kdec = sb("p2_kdec", [128, E], F32)
                c_kdec = K.ctr("p2_kdec_ld")
                K.emit("sync", lambda e: e.dma_start(out=t_kdec[:], in_=d_kdec[:, :]), inc=[(c_kdec, 16)])
                kpst = [sb("k_kp0", [128, 1024], BF16), sb("k_kp1", [128, 1024], BF16)]
                c_sp2 = [K.ctr("k_sp20"), K.ctr("k_sp21")]
        else:
            vst = [sb("v_st0", [128, 1024], BF16), sb("v_st1", [128, 1024], BF16)]
            kpb = [sb("v_kpb0", [128, 1024], BF16), sb("v_kpb1", [128, 1024], BF16)]
            pkv = [ps("v_pkv0", [128, 512], F32), ps("v_pkv1", [128, 512], F32)]
            c_kp = [K.ctr("v_kp0"), K.ctr("v_kp1")]
            tok_upd = {}
        Wr = W.rearrange("(kc p) n -> p kc n", p=128)
        pending = [None]
        for q2 in range(2):
            w = [K.pe.tok()] if q2 > 0 else []
            K.dma_batch("gpsimd", c_ldw, [
                (lambda e, q2=q2, k4=k4: e.dma_start(
                    out=w4[:, 4 * k4:4 * (k4 + 1), :],
                    in_=Wr[:, 4 * k4:4 * (k4 + 1), q2 * 1024:(q2 + 1) * 1024]))
                for k4 in range(4)], wait=w)
            ldwv = c_ldw.tok()
            for g in range(G):
                p = g % 2
                gi = q2 * G + g
                def f_mm(e, p=p, g=g):
                    last = None
                    for kc in range(16):
                        e.matmul(pp[p][0][:], t_xlnT[:, kc, g * 128:(g + 1) * 128],
                                 w4[:, kc, 0:512], start=(kc == 0), stop=(kc == 15))
                        last = e.matmul(pp[p][1][:], t_xlnT[:, kc, g * 128:(g + 1) * 128],
                                        w4[:, kc, 512:1024], start=(kc == 0), stop=(kc == 15))
                    return last
                w = [ldwv]
                if gi - 2 in tok_ev:
                    w.append((K.act, tok_ev[gi - 2]))
                tok_mm[gi] = K.emit("tensor", f_mm, wait=w, inc=[(K.pe, 1)])
                if which in ("q", "k"):
                    w = [(K.dve, tok_rot[gi - 2])] if gi - 2 in tok_rot else []
                    K.dma_batch("sync", c_cs[p], [
                        lambda e, p=p, g=g: e.dma_start(out=cs[p][:], in_=d_cos[g, :, :]),
                        lambda e, p=p, g=g: e.dma_start(out=sn[p][:], in_=d_sin[g, :, :])],
                        wait=w)
                    w = [(K.pe, tok_mm[gi])]
                    if gi - 2 in tok_tr:
                        w.append((K.pe, tok_tr[gi - 2]))
                    def f_ev(e, p=p):
                        e.activation(qtm[p][:, 0:512], pp[p][0][:], AF.Identity)
                        return e.activation(qtm[p][:, 512:1024], pp[p][1][:], AF.Identity)
                    tok_ev[gi] = K.emit("scalar", f_ev, wait=w, inc=[(K.act, 1)])
                    v1 = K.dve.reserve(1)
                    vh = K.dve.reserve(1) if which == "k" else None
                    def f_rot(e, p=p, q2=q2, v1=v1, vh=vh):
                        a = qtm[p].ap().rearrange("t (h d) -> t h d", h=8)
                        x1, x2 = a[:, :, 0:32], a[:, :, 64:96]
                        c4 = cs[p].ap().rearrange("t (h j) -> t h j", h=8)
                        s4 = sn[p].ap().rearrange("t (h j) -> t h j", h=8)
                        r4 = lambda v: v.rearrange("t (h j) -> t h j", h=8)
                        e.tensor_mul(r4(tmp[:, 0, :]), x1, c4)
                        e.tensor_mul(r4(tmp[:, 1, :]), x1, s4)
                        e.tensor_mul(r4(tmp[:, 2, :]), x2, s4)
                        e.tensor_mul(r4(tmp[:, 3, :]), x2, c4).then_inc(K.dve.h, 1)
                        e.wait_ge(K.dve.h, v1)
                        e.tensor_add(x1, r4(tmp[:, 0, :]), r4(tmp[:, 2, :]))
                        last = e.tensor_sub(x2, r4(tmp[:, 3, :]), r4(tmp[:, 1, :]))
                        if which == "k":
                            last.then_inc(K.dve.h, 1)
                            e.wait_ge(K.dve.h, vh)
                            last = e.tensor_mul(kpst[p][:], qtm[p][:], t_kdec[:, q2 * 1024:(q2 + 1) * 1024])
                        return last
                    w = [(K.act, tok_ev[gi]), c_cs[p].tok()]
                    if which == "k":
                        w.append(c_kdec.tok())
                        if gi - 2 in tok_rot:
                            w.append(c_sp2[p].tok())
                    tok_rot[gi] = K.emit("vector", f_rot, wait=w, inc=[(K.dve, 1)],
                                         rd=[qtm[p], cs[p], sn[p]],
                                         wr=[qtm[p], tmp] + ([kpst[p]] if which == "k" else []))
                    if which == "k":
                        K.emit("sync", lambda e, p=p, g=g, q2=q2: e.dma_start(
                            out=d_kp[g, :, q2 * 1024:(q2 + 1) * 1024], in_=kpst[p][:]),
                            wait=[(K.dve, tok_rot[gi])], inc=[(c_sp2[p], 16)])
                    def emit_tail(gi=gi, p=p, g=g, q2=q2):
                        def f_tr(e, p=p):
                            last = None
                            for j in range(8):
                                bank = tp[j // 4]
                                last = e.matmul(bank[:, (j % 4) * 128:(j % 4 + 1) * 128],
                                                qtm[p][:, j * 128:(j + 1) * 128], t_id32[:],
                                                is_transpose=True, start=(j % 4 == 0), stop=(j % 4 == 3),
                                                skip_group_check=True)
                            return last
                        w = [(K.dve, tok_rot[gi])]
                        if gi - 1 in tok_st:
                            w.append((K.dve, tok_st[gi - 1]))
                        tok_tr[gi] = K.emit("tensor", f_tr, wait=w, inc=[(K.pe, 1)])
                        w = [(K.pe, tok_tr[gi]), c_sp[p].tok()]
                        def f_st(e, p=p):
                            e.tensor_copy(stg[p][:, 0:512], tp[0][:])
                            return e.tensor_copy(stg[p][:, 512:1024], tp[1][:])
                        tok_st[gi] = K.emit("vector", f_st, wait=w, inc=[(K.dve, 1)], rd=[], wr=[stg[p]])
                        dst = d_qfm if which == "q" else d_kfm
                        K.emit("sync", lambda e, p=p, g=g, q2=q2, dst=dst: e.dma_start(
                            out=dst[g, :, q2 * 1024:(q2 + 1) * 1024], in_=stg[p][:]),
                            wait=[(K.dve, tok_st[gi])], inc=[(c_sp[p], 16)])
                    if pending[0] is not None:
                        pending[0]()
                    pending[0] = emit_tail
                else:
                    w = [(K.pe, tok_mm[gi]), c_sp[p].tok()]
                    if gi - 2 in tok_tr:
                        w.append((K.pe, tok_tr[gi - 2]))
                    def f_ev(e, p=p):
                        e.activation(vst[p][:, 0:512], pp[p][0][:], AF.Identity)
                        return e.activation(vst[p][:, 512:1024], pp[p][1][:], AF.Identity)
                    tok_ev[gi] = K.emit("scalar", f_ev, wait=w, inc=[(K.act, 1)])
                    K.emit("sync", lambda e, p=p, g=g, q2=q2: e.dma_start(
                        out=d_vs[g, :, q2 * 1024:(q2 + 1) * 1024], in_=vst[p][:]),
                        wait=[(K.act, tok_ev[gi])], inc=[(c_sp[p], 16)])
                    w = [(K.pe, tok_tr[gi - 2])] if gi - 2 in tok_tr else []
                    K.emit("sync", lambda e, p=p, g=g, q2=q2: e.dma_start(
                        out=kpb[p][:], in_=d_kp[g, :, q2 * 1024:(q2 + 1) * 1024]),
                        wait=w, inc=[(c_kp[p], 16)])
                    def emit_tail(gi=gi, p=p, q2=q2):
                        def f_kvu(e, p=p):
                            last = None
                            for j in range(8):
                                bank = pkv[j // 4]
                                last = e.matmul(bank[:, (j % 4) * 128:(j % 4 + 1) * 128],
                                                kpb[p][:, j * 128:(j + 1) * 128],
                                                vst[p][:, j * 128:(j + 1) * 128],
                                                start=(j % 4 == 0), stop=(j % 4 == 3), skip_group_check=True)
                            return last
                        w = [(K.act, tok_ev[gi]), c_kp[p].tok()]
                        if gi - 1 in tok_upd:
                            w.append((K.dve, tok_upd[gi - 1]))
                        tok_tr[gi] = K.emit("tensor", f_kvu, wait=w, inc=[(K.pe, 1)])
                        def f_upd(e, p=p, q2=q2):
                            last = None
                            for j in range(8):
                                h = q2 * 8 + j
                                last = e.scalar_tensor_tensor(
                                    t_kv[:, h * 128:(h + 1) * 128], t_kv[:, h * 128:(h + 1) * 128],
                                    rC[h], pkv[j // 4][:, (j % 4) * 128:(j % 4 + 1) * 128], ALU.mult, ALU.add)
                            return last
                        tok_upd[gi] = K.emit("vector", f_upd, wait=[(K.pe, tok_tr[gi])], inc=[(K.dve, 1)],
                                             rd=[t_kv], wr=[t_kv])
                    if pending[0] is not None:
                        pending[0]()
                    pending[0] = emit_tail
        if pending[0] is not None:
            pending[0]()
            pending[0] = None
        drains = [c_sp[0].tok(), c_sp[1].tok()]
        if which == "k":
            drains += [c_sp2[0].tok(), c_sp2[1].tok()]
        K.emit("sync", None, wait=drains)
        K.barrier()

    def px():
        nc.sbuf_base = epoch_sbuf
        t_wvec = sb("x_wvec", [128, 128], F32)
        kvg = [sb("x_kvg0", [128, E], BF16), sb("x_kvg1", [128, E], BF16)]
        c_cc = K.ctr("x_cc")
        c_xi = K.ctr("x_xi")
        c_xg = [K.ctr("x_xg0"), K.ctr("x_xg1")]
        c_db = K.ctr("x_db")
        c_xw = K.ctr("x_xw")
        K.emit("sync", lambda e: e.dma_start(out=t_wvec[:], in_=d_wvec.rearrange("p j h -> p (j h)")),
               inc=[(c_xw, 16)])
        if dbg:
            K.emit("sync", lambda e: e.dma_start(out=d_kvl[:, :], in_=t_kv[:]), inc=[(c_db, 16)])
        t_cast = K.emit("scalar", lambda e: e.activation(t_kvbf[:], t_kv[:], AF.Identity),
                        inc=[(K.act, 1)])
        K.emit("gpsimd", lambda e: e.dma_start(
            out=d_ccin.ap().rearrange("(h d) e2 -> d h e2", h=NH),
            in_=t_kvbf.ap().rearrange("p (h e2) -> p h e2", h=NH)),
            wait=[(K.act, t_cast)], inc=[(c_xi, 16)])
        K.emit("gpsimd", lambda e: e.collective_compute(
            "AllGather", ALU.bypass, replica_groups=[list(range(8))],
            ins=[d_ccin.ap().opt()], outs=[d_ccout.ap().opt()]),
            wait=[c_xi.tok()] + ([c_db.tok()] if dbg else []), inc=[(c_cc, 1)])
        toks = []
        for j in range(8):
            p = j % 2
            w = [c_cc.tok()]
            if j >= 2:
                w.append((K.dve, toks[j - 2]))
            K.emit("gpsimd", lambda e, j=j, p=p: e.dma_start(
                out=kvg[p].ap().rearrange("p2 (h e2) -> p2 h e2", h=NH),
                in_=d_ccout[j * NH * HD:(j + 1) * NH * HD, :].rearrange("(h d) e2 -> d h e2", h=NH)),
                wait=w, inc=[(c_xg[p], 16)])
            def f_comb(e, j=j, p=p):
                last = None
                for h in range(NH):
                    if j == 0:
                        last = e.tensor_scalar_mul(t_kv[:, h * 128:(h + 1) * 128],
                                                   kvg[p][:, h * 128:(h + 1) * 128],
                                                   t_wvec[:, j * 16 + h:j * 16 + h + 1])
                    else:
                        last = e.scalar_tensor_tensor(
                            t_kv[:, h * 128:(h + 1) * 128], kvg[p][:, h * 128:(h + 1) * 128],
                            t_wvec[:, j * 16 + h:j * 16 + h + 1], t_kv[:, h * 128:(h + 1) * 128],
                            ALU.mult, ALU.add)
                return last
            toks.append(K.emit("vector", f_comb, wait=[c_xg[p].tok(), c_xw.tok()], inc=[(K.dve, 1)],
                               rd=[kvg[p], t_kv], wr=[t_kv]))
        K.emit("vector", lambda e: e.tensor_copy(t_kvbf[:], t_kv[:]),
               wait=[(K.dve, toks[-1])], inc=[(K.dve, 1)], rd=[t_kv], wr=[t_kvbf])
        if dbg:
            K.emit("sync", lambda e: e.dma_start(out=d_kvin[:, :], in_=t_kv[:]),
                   wait=[K.dve.tok()], inc=[(c_db, 16)])
        K.barrier()

    def p4():
        nc.sbuf_base = base_kv
        nc.psum_base = base_psum
        wob = sb("p4_wob", [128, 16, E], BF16)
        t_dmT = sb("p4_dmT", [128, NH * 128], F32)
        t_rdec = sb("p4_rdec", [128, 16], F32)
        t_rmsw = sb("p4_rmsw", [128, 16], F32)
        qc = [sb(f"p4_q{i}", [128, E], BF16) for i in range(3)]
        kc = [sb(f"p4_k{i}", [128, E], BF16) for i in range(3)]
        kpc = [sb(f"p4_kp{i}", [128, E], BF16) for i in range(3)]
        vc = [sb(f"p4_v{i}", [128, E], BF16) for i in range(3)]
        xc = [sb("p4_x0", [128, E], F32), sb("p4_x1", [128, E], F32)]
        scm = sb("p4_scm", [128, E], BF16)
        isb = sb("p4_isb", [128, E], F32)
        attn = [sb("p4_at0", [128, E], F32), sb("p4_at1", [128, E], F32)]
        rmst = sb("p4_rmst", [128, E], BF16)
        rmsf = sb("p4_rmsf", [128, 16, 128], BF16)
        x2 = sb("p4_x2", [128, E], F32)
        xl2 = sb("p4_xl2", [128, E], BF16)
        stg2 = sb("p4_stg2", [128, 16, 128], BF16)
        stats = sb("p4_stats", [128, 24], F32)
        mv = sb("p4_mv", [128, 2], F32)
        srt = sb("p4_srt", [128, 1], F32)
        rstd = sb("p4_rstd", [128, 1], F32)
        nmb = sb("p4_nmb", [128, 1], F32)
        msq = sb("p4_msq", [128, 1], F32)
        psA = ps("p4_psA", [128, 1024], F32)
        psB = ps("p4_psB", [128, 1024], F32)
        psT = ps("p4_psT", [128, 2048], BF16)
        psW = [ps("p4_psW0", [128, 512], F32), ps("p4_psW1", [128, 512], F32)]
        c_ld = [K.ctr("p4_ld0"), K.ctr("p4_ld1"), K.ctr("p4_ld2")]
        c_lx = [K.ctr("p4_lx0"), K.ctr("p4_lx1")]
        c_stx = K.ctr("p4_stx")
        c_sts = K.ctr("p4_sts")
        c_wo = K.ctr("p4_wo")
        c_wo2 = K.ctr("p4_wo2")
        K.dma_batch("sync", c_wo2, [
            lambda e: e.dma_start(out=t_dmT.ap().rearrange("s (h c) -> s h c", h=NH), in_=d_dmT.rearrange("h s c -> s h c")),
            lambda e: e.dma_start(out=t_rdec[:], in_=d_rdec[:, :]),
            lambda e: e.dma_start(out=t_rmsw[:], in_=d_rmsw[:, :]),
        ])
        Wor = d_wo.rearrange("(c p) n -> p c n", p=128)
        K.dma_batch("gpsimd", c_wo, [
            (lambda e, k4=k4: e.dma_start(out=wob[:, 4 * k4:4 * (k4 + 1), :],
                                          in_=Wor[:, 4 * k4:4 * (k4 + 1), :]))
            for k4 in range(4)])
        def f_scalew(e):
            last = None
            for cc in range(16):
                last = e.tensor_scalar_mul(wob[:, cc, :], wob[:, cc, :], t_rmsw[:, cc:cc + 1])
            return last
        tok_wo = K.emit("vector", f_scalew, wait=[c_wo.tok(), c_wo2.tok()], inc=[(K.dve, 1)], rd=[wob], wr=[wob])

        scan_t, tail_t = {}, {}

        def emit_scan(c):
            p = c % 3
            px = c % 2
            pa = c % 2
            T = {}
            w = []
            if c - 3 in scan_t:
                w.append((K.pe, scan_t[c - 3]["kvu1"]))
            K.dma_batch("sync", c_ld[p], [
                lambda e, c=c, p=p: e.dma_start(out=qc[p][:], in_=d_qfm[c, :, :]),
                lambda e, c=c, p=p: e.dma_start(out=kc[p][:], in_=d_kfm[c, :, :]),
                lambda e, c=c, p=p: e.dma_start(out=kpc[p][:], in_=d_kp[c, :, :]),
                lambda e, c=c, p=p: e.dma_start(out=vc[p][:], in_=d_vs[c, :, :]),
            ], wait=w)
            w = []
            if c - 2 in tail_t:
                w.append((K.dve, tail_t[c - 2]["woe3"]))
            K.dma_batch("sync", c_lx[px], [
                lambda e, c=c, px=px: e.dma_start(out=xc[px][:], in_=d_x[c * 128:(c + 1) * 128, :]),
            ], wait=w)
            ldv = c_ld[p].tok()
            for hb in range(2):
                h0 = hb * 8
                def f_sc(e, p=p, h0=h0):
                    last = None
                    for j in range(8):
                        h = h0 + j
                        last = e.matmul(psA[:, j * 128:(j + 1) * 128], kc[p][:, h * 128:(h + 1) * 128],
                                        qc[p][:, h * 128:(h + 1) * 128], start=(j % 4 == 0), stop=(j % 4 == 3),
                                        skip_group_check=True)
                    return last
                w = [ldv]
                if hb == 1:
                    w.append((K.dve, T["cmb0"]))
                elif c - 1 in scan_t:
                    w.append((K.dve, scan_t[c - 1]["cmb1"]))
                if c - 1 in scan_t:
                    w.append((K.dve, scan_t[c - 1]["upd"]))
                t_sc = K.emit("tensor", f_sc, wait=w, inc=[(K.pe, 1)])
                t_mask = K.emit("vector",
                    lambda e, h0=h0: e.tensor_mul(scm[:, h0 * 128:(h0 + 8) * 128], psA[:],
                                                  t_dmT[:, h0 * 128:(h0 + 8) * 128]),
                    wait=[(K.pe, t_sc), c_wo2.tok()], inc=[(K.dve, 1)], rd=[], wr=[scm])
                def f_ii(e, p=p, h0=h0):
                    last = None
                    for j in range(8):
                        h = h0 + j
                        last = e.matmul(psA[:, j * 128:(j + 1) * 128], scm[:, h * 128:(h + 1) * 128],
                                        vc[p][:, h * 128:(h + 1) * 128], start=(j % 4 == 0), stop=(j % 4 == 3),
                                        skip_group_check=True)
                    for j in range(8):
                        h = h0 + j
                        last = e.matmul(psB[:, j * 128:(j + 1) * 128], qc[p][:, h * 128:(h + 1) * 128],
                                        t_kvbf[:, h * 128:(h + 1) * 128], start=(j % 4 == 0), stop=(j % 4 == 3),
                                        skip_group_check=True)
                    return last
                w = [(K.dve, t_mask)]
                if hb == 1:
                    w.append((K.dve, T["upd0"]))
                elif c - 1 in scan_t:
                    w.append((K.dve, scan_t[c - 1]["upd"]))
                t_ii = K.emit("tensor", f_ii, wait=w, inc=[(K.pe, 1)])
                def f_iev(e, h0=h0):
                    last = None
                    for j in range(8):
                        h = h0 + j
                        last = e.tensor_scalar_mul(isb[:, h * 128:(h + 1) * 128],
                                                   psB[:, j * 128:(j + 1) * 128], t_rdec[:, h:h + 1])
                    return last
                t_iev = K.emit("vector", f_iev, wait=[(K.pe, t_ii)], inc=[(K.dve, 1)],
                               rd=[], wr=[isb])
                w = [(K.dve, t_iev)]
                if c - 2 in tail_t:
                    w.append((K.act, tail_t[c - 2]["rap"]))
                t_cmb = K.emit("vector",
                    lambda e, pa=pa, h0=h0: e.tensor_add(attn[pa][:, h0 * 128:(h0 + 8) * 128], psA[:],
                                                         isb[:, h0 * 128:(h0 + 8) * 128]),
                    wait=w, inc=[(K.dve, 1)], rd=[isb], wr=[attn[pa]])
                def f_kvu(e, p=p, h0=h0):
                    last = None
                    for j in range(8):
                        h = h0 + j
                        last = e.matmul(psB[:, j * 128:(j + 1) * 128], kpc[p][:, h * 128:(h + 1) * 128],
                                        vc[p][:, h * 128:(h + 1) * 128], start=(j % 4 == 0), stop=(j % 4 == 3),
                                        skip_group_check=True)
                    return last
                t_kvu = K.emit("tensor", f_kvu, wait=[(K.dve, t_iev)], inc=[(K.pe, 1)])
                vh = K.dve.reserve(1) if hb == 1 else None
                def f_upd(e, h0=h0, hb=hb, vh=vh):
                    last = None
                    for j in range(8):
                        h = h0 + j
                        last = e.scalar_tensor_tensor(
                            t_kv[:, h * 128:(h + 1) * 128], t_kv[:, h * 128:(h + 1) * 128],
                            rC[h], psB[:, j * 128:(j + 1) * 128], ALU.mult, ALU.add)
                    if hb == 1:
                        last.then_inc(K.dve.h, 1)
                        e.wait_ge(K.dve.h, vh)
                        last = e.tensor_copy(t_kvbf[:], t_kv[:])
                    return last
                t_upd = K.emit("vector", f_upd, wait=[(K.pe, t_kvu)], inc=[(K.dve, 1)],
                               rd=[t_kv], wr=[t_kv, t_kvbf])
                if hb == 0:
                    T["cmb0"], T["upd0"] = t_cmb, t_upd
                else:
                    T["cmb1"], T["upd"] = t_cmb, t_upd
                    T["kvu1"] = t_kvu
            scan_t[c] = T

        def emit_tail(c):
            p = c % 2
            T = {}
            v1, v2, v3 = K.dve.reserve(1), K.dve.reserve(1), K.dve.reserve(1)
            def f_rstats(e, p=p, v1=v1, v2=v2, v3=v3):
                st = None
                for c4 in range(4):
                    st = e.bn_stats(stats[:, 6 * c4:6 * c4 + 6], attn[p][:, 512 * c4:512 * (c4 + 1)])
                st.then_inc(K.dve.h, 1)
                e.wait_ge(K.dve.h, v1)
                e.bn_aggr(mv[:], stats[:, 0:24]).then_inc(K.dve.h, 1)
                e.wait_ge(K.dve.h, v2)
                e.tensor_mul(msq[:], mv[:, 0:1], mv[:, 0:1]).then_inc(K.dve.h, 1)
                e.wait_ge(K.dve.h, v3)
                return e.tensor_add(msq[:], msq[:], mv[:, 1:2])
            w = [(K.dve, scan_t[c]["cmb1"])]
            if c - 1 in tail_t:
                w.append((K.act, tail_t[c - 1]["ln2ap"]))
            t_rs = K.emit("vector", f_rstats, wait=w, inc=[(K.dve, 1)],
                          rd=[attn[p]], wr=[stats, mv, msq])
            va = K.act.reserve(1)
            def f_rsq(e, va=va):
                e.activation(srt[:], msq[:], AF.Ln, bias=t_eps5[:]).then_inc(K.act.h, 1)
                e.wait_ge(K.act.h, va)
                return e.activation(rstd[:], srt[:], AF.Exp, scale=t_nhalf[:, 0:1])
            t_sq = K.emit("scalar", f_rsq, wait=[(K.dve, t_rs)], inc=[(K.act, 1)])
            w = [(K.act, t_sq)]
            if c - 1 in tail_t:
                w.append((K.pe, tail_t[c - 1]["tr1"]))
            T["rap"] = K.emit("scalar", lambda e, p=p: e.activation(rmst[:], attn[p][:], AF.Identity, scale=rstd[:]),
                              wait=w, inc=[(K.act, 1)])
            def f_tr1(e):
                last = None
                for cc in range(16):
                    last = e.matmul(psT[:, cc * 128:(cc + 1) * 128], rmst[:, cc * 128:(cc + 1) * 128],
                                    t_idbf[:], is_transpose=True, start=(cc % 8 == 0), stop=(cc % 8 == 7),
                                    skip_group_check=True)
                return last
            w = [(K.act, T["rap"])]
            if c - 1 in tail_t:
                w.append((K.dve, tail_t[c - 1]["ev2"]))
            T["tr1"] = K.emit("tensor", f_tr1, wait=w, inc=[(K.pe, 1)])
            t_ev1 = K.emit("vector", lambda e: e.tensor_copy(rmsf[:], psT.ap().rearrange("p (c t) -> p c t", c=16)),
                           wait=[(K.pe, T["tr1"])], inc=[(K.dve, 1)], rd=[], wr=[rmsf])
            t_woe = {}
            for half in range(2):
                nb0, nb1 = 2 * half, 2 * half + 1
                def f_wo(e, nb0=nb0, nb1=nb1):
                    last = None
                    for cc in range(16):
                        e.matmul(psW[0][:], rmsf[:, cc, :], wob[:, cc, nb0 * 512:(nb0 + 1) * 512],
                                 start=(cc == 0), stop=(cc == 15))
                        last = e.matmul(psW[1][:], rmsf[:, cc, :], wob[:, cc, nb1 * 512:(nb1 + 1) * 512],
                                        start=(cc == 0), stop=(cc == 15))
                    return last
                w = [(K.dve, t_ev1), (K.dve, tok_wo)]
                if half == 1:
                    w.append((K.dve, t_woe[1]))
                elif c - 1 in tail_t:
                    w.append((K.dve, tail_t[c - 1]["woe3"]))
                t_mmwo = K.emit("tensor", f_wo, wait=w, inc=[(K.pe, 1)])
                w = [(K.pe, t_mmwo), c_lx[c % 2].tok()]
                if c - 1 in tail_t and half == 0:
                    w.append(c_stx.tok())
                    w.append((K.act, tail_t[c - 1]["ln2ap"]))
                def f_add(e, nb0=nb0, nb1=nb1, p=p):
                    e.tensor_add(x2[:, nb0 * 512:(nb0 + 1) * 512], xc[p][:, nb0 * 512:(nb0 + 1) * 512], psW[0][:])
                    return e.tensor_add(x2[:, nb1 * 512:(nb1 + 1) * 512], xc[p][:, nb1 * 512:(nb1 + 1) * 512], psW[1][:])
                t_woe[nb1] = t_woe[nb0] = K.emit("vector", f_add, wait=w, inc=[(K.dve, 1)],
                                                 rd=[xc[p]], wr=[x2])
            T["woe3"] = t_woe[3]
            K.emit("sync", lambda e, c=c: e.dma_start(out=d_out[c * 128:(c + 1) * 128, :], in_=x2[:]),
                   wait=[(K.dve, t_woe[3])], inc=[(c_stx, 16)])
            v1, v2 = K.dve.reserve(1), K.dve.reserve(1)
            def f_l2s(e, v1=v1, v2=v2):
                st = None
                for c4 in range(4):
                    st = e.bn_stats(stats[:, 6 * c4:6 * c4 + 6], x2[:, 512 * c4:512 * (c4 + 1)])
                st.then_inc(K.dve.h, 1)
                e.wait_ge(K.dve.h, v1)
                e.bn_aggr(mv[:], stats[:, 0:24]).then_inc(K.dve.h, 1)
                e.wait_ge(K.dve.h, v2)
                return e.tensor_scalar_mul(nmb[:], mv[:, 0:1], -1.0)
            t_l2s = K.emit("vector", f_l2s, wait=[(K.dve, t_woe[3]), (K.act, t_sq)], inc=[(K.dve, 1)],
                           rd=[x2], wr=[stats, mv, nmb])
            va = K.act.reserve(1)
            def f_l2q(e, va=va):
                e.activation(srt[:], mv[:, 1:2], AF.Ln, bias=t_eps5[:]).then_inc(K.act.h, 1)
                e.wait_ge(K.act.h, va)
                return e.activation(rstd[:], srt[:], AF.Exp, scale=t_nhalf[:, 0:1])
            t_l2q = K.emit("scalar", f_l2q, wait=[(K.dve, t_l2s)], inc=[(K.act, 1)])
            t_l2r = K.emit("vector", lambda e: e.tensor_mul(nmb[:], nmb[:], rstd[:]),
                           wait=[(K.act, t_l2q)], inc=[(K.dve, 1)], rd=[rstd, nmb], wr=[nmb])
            w = [(K.dve, t_l2r)]
            if c - 1 in tail_t:
                w.append((K.pe, tail_t[c - 1]["tr2"]))
            T["ln2ap"] = K.emit("scalar",
                lambda e: e.activation(xl2[:], x2[:], AF.Identity, scale=rstd[:], bias=nmb[:]),
                wait=w, inc=[(K.act, 1)])
            def f_tr2(e):
                last = None
                for cc in range(16):
                    last = e.matmul(psT[:, cc * 128:(cc + 1) * 128], xl2[:, cc * 128:(cc + 1) * 128],
                                    t_idbf[:], is_transpose=True, start=(cc % 8 == 0), stop=(cc % 8 == 7),
                                    skip_group_check=True)
                return last
            T["tr2"] = K.emit("tensor", f_tr2, wait=[(K.act, T["ln2ap"]), (K.dve, t_ev1)], inc=[(K.pe, 1)])
            w = [(K.pe, T["tr2"]), c_sts.tok()]
            T["ev2"] = K.emit("vector", lambda e: e.tensor_copy(stg2[:], psT.ap().rearrange("p (c t) -> p c t", c=16)),
                              wait=w, inc=[(K.dve, 1)], rd=[], wr=[stg2])
            K.emit("sync", lambda e, c=c: e.dma_start(
                out=d_x2T[:, :, c * 128:(c + 1) * 128].rearrange("k f t -> f k t"), in_=stg2[:]),
                wait=[(K.dve, T["ev2"])], inc=[(c_sts, 16)])
            T["done"] = T["ev2"]
            tail_t[c] = T

        for c in range(G):
            emit_scan(c)
            if c > 0:
                emit_tail(c - 1)
        emit_tail(G - 1)
        K.emit("sync", None, wait=[c_stx.tok(), c_sts.tok()])
        K.barrier()

    def p5():
        nc.sbuf_base = base_sbuf
        nc.psum_base = base_psum
        NTB = G // 4
        xpn = sb("p5_xpn", [128, 16, Tc], BF16)
        wg = sb("p5_wg", [128, 16, 256], BF16)
        wqb = [sb(f"p5_wq{i}", [128, 16, 256], BF16) for i in range(3)]
        wabs = sb("p5_wabs", [128, 4096], BF16)
        sgn = sb("p5_sgn", [128, 4096], BF16)
        srep = sb("p5_srep", [128, 16, 256], BF16)
        sbf = [sb(f"p5_sb{i}", [2, 512], BF16) for i in range(4)]
        sil = [sb("p5_si0", [128, 512], BF16), sb("p5_si1", [128, 512], BF16)]
        hst = [sb("p5_h0", [128, 512], BF16), sb("p5_h1", [128, 512], BF16)]
        wpin = sb("p5_wpin", [128, E], BF16)
        wpo = sb("p5_wpo", [128, E], BF16)
        qg = sb("p5_qg", [128, E], BF16)
        qsb = [sb("p5_qsb0", [2, 512], BF16), sb("p5_qsb1", [2, 512], BF16)]
        psg = [ps("p5_psg0", [128, 512], F32), ps("p5_psg1", [128, 512], F32)]
        psu = [ps("p5_psu0", [128, 512], F32), ps("p5_psu1", [128, 512], F32)]
        pss = [ps("p5_pss0", [128, 512], F32), ps("p5_pss1", [128, 512], F32)]
        psq = [ps("p5_psq0", [128, 512], F32), ps("p5_psq1", [128, 512], F32)]
        c_pn = K.ctr("p5_pn")
        c_ldg = K.ctr("p5_ldg")
        c_sth = [K.ctr("p5_sth0"), K.ctr("p5_sth1")]
        c_ldq = [K.ctr("p5_ldq0"), K.ctr("p5_ldq1")]
        c_stq = K.ctr("p5_stq0")
        pnq = []
        for q in range(4):
            K.dma_batch("sync", c_pn,
                        [(lambda e, kc=kc, q=q: e.dma_start(
                            out=xpn[:, kc, q * 1024:(q + 1) * 1024],
                            in_=d_x2T[kc, :, q * 1024:(q + 1) * 1024]))
                         for kc in range(16)])
            pnq.append(c_pn.tok())
        pnv = pnq[-1]
        Wgr = d_wguT.rearrange("(kc p) n -> p kc n", p=128)
        quant_t, wpq_t, gu_h = {}, {}, {}

        def quant_start(m):
            p = m % 3
            T = {"m": m}
            w = []
            if m - 1 in quant_t:
                w.append((K.act, quant_t[m - 1]["sgn"]))
            K.dma_batch("gpsimd", c_ldg, [
                (lambda e, m=m, k8=k8: e.dma_start(
                    out=wg[:, 8 * k8:8 * (k8 + 1), 0:128],
                    in_=Wgr[:, 8 * k8:8 * (k8 + 1), m * 128:(m + 1) * 128]))
                for k8 in range(2)] + [
                (lambda e, m=m, k8=k8: e.dma_start(
                    out=wg[:, 8 * k8:8 * (k8 + 1), 128:256],
                    in_=Wgr[:, 8 * k8:8 * (k8 + 1), HFF + m * 128:HFF + (m + 1) * 128]))
                for k8 in range(2)], wait=w)
            ldv = c_ldg.tok()
            w = [ldv]
            if m - 1 in quant_t:
                w.append((K.dve, quant_t[m - 1]["tt"]))
            wgf = wg.ap().rearrange("p k c -> p (k c)")
            T["abs"] = K.emit("scalar", lambda e, wgf=wgf: e.activation(wabs[:], wgf, AF.Abs, scale=t_two[:, 0:1]),
                              wait=w, inc=[(K.act, 1)])
            T["sgn"] = K.emit("scalar", lambda e, wgf=wgf: e.activation(sgn[:], wgf, AF.Sign),
                              wait=[], inc=[(K.act, 1)])
            T["ev"] = {}
            quant_t[m] = T
            return T

        ring = [pss[0], pss[1], psq[0], psq[1]]
        rotc = [0]
        rg = {}

        def quant_slice_a(T, kk2):
            bA, bB = rotc[0] % 4, (rotc[0] + 1) % 4
            rotc[0] += 2
            T["bk", kk2] = (bA, bB)
            wk = [(K.act, T["abs"])]
            if bA in rg:
                wk.append(rg[bA])
            t_smm = K.emit("tensor",
                lambda e, kk2=kk2, bA=bA: e.matmul(ring[bA][0:2, 0:512], t_E2[:],
                                                   wabs[:, kk2 * 512:(kk2 + 1) * 512], start=True, stop=True,
                                                   skip_group_check=True),
                wait=wk, inc=[(K.pe, 1)])
            rp = kk2 % 4
            wc = [(K.pe, t_smm)]
            if ("bmm", kk2 - 4) in T:
                wc.append((K.pe, T["bmm", kk2 - 4]))
            T["clip", kk2] = K.emit("vector",
                lambda e, rp=rp, bA=bA: e.tensor_scalar_max(sbf[rp][:], ring[bA][0:2, 0:512], 1e-8),
                wait=wc, inc=[(K.dve, 1)], rd=[], wr=[sbf[rp]])
            rg[bA] = (K.dve, T["clip", kk2])

        def quant_slice_b(T, kk2):
            bA, bB = T["bk", kk2]
            rp = kk2 % 4
            wb = [(K.dve, T["clip", kk2])]
            if bB in rg:
                wb.append(rg[bB])
            t_bmm = K.emit("tensor",
                lambda e, rp=rp, bB=bB: e.matmul(ring[bB][:, 0:512], t_B2[:], sbf[rp][:], start=True, stop=True,
                                                 skip_group_check=True),
                wait=wb, inc=[(K.pe, 1)])
            T["bmm", kk2] = t_bmm
            T["ev"][kk2] = K.emit("scalar",
                lambda e, kk2=kk2, bB=bB: e.activation(srep[:, 2 * kk2:2 * kk2 + 2, :],
                                                       ring[bB].ap().rearrange("p (a b) -> p a b", a=2),
                                                       AF.Identity),
                wait=[(K.pe, t_bmm)], inc=[(K.act, 1)])
            rg[bB] = (K.act, T["ev"][kk2])

        def quant_tail(T):
            m = T["m"]
            p = m % 3
            srf = srep.ap().rearrange("p k c -> p (k c)")
            wqf = wqb[p].ap().rearrange("p k c -> p (k c)")
            w = [(K.act, T["ev"][7]), (K.act, T["sgn"])]
            if m - 3 in quant_t:
                w.append((K.pe, quant_t[m - 3]["lastmm"]))
            K.emit("vector", lambda e, wqf=wqf, srf=srf: e.tensor_tensor(wqf, wabs[:], srf, ALU.is_gt),
                   wait=w, inc=[(K.dve, 1)], rd=[wabs, srep], wr=[wqb[p]])
            K.emit("vector", lambda e, wqf=wqf, srf=srf: e.tensor_tensor(wqf, wqf, srf, ALU.mult),
                   wait=[], inc=[(K.dve, 1)], rd=[wqb[p], srep], wr=[wqb[p]])
            T["tt"] = K.emit("vector", lambda e, wqf=wqf: e.tensor_tensor(wqf, wqf, sgn[:], ALU.mult),
                             wait=[], inc=[(K.dve, 1)], rd=[wqb[p], sgn], wr=[wqb[p]])

        def wpq_start(j):
            p = j % 2
            T = {"j": j, "ev": {}}
            w = []
            if j - 1 in wpq_t:
                w.append((K.act, wpq_t[j - 1]["sgn"]))
                w.append((K.dve, wpq_t[j - 1]["tt"]))
            K.emit("gpsimd", lambda e, j=j: e.dma_start(out=wpin[:], in_=d_wpT[j * 128:(j + 1) * 128, :]),
                   wait=w, inc=[(c_ldq[p], 16)])
            w = [c_ldq[p].tok()]
            if j - 1 in wpq_t:
                w.append((K.dve, wpq_t[j - 1]["tt"]))
            T["abs"] = K.emit("scalar", lambda e: e.activation(wpo[:], wpin[:], AF.Abs, scale=t_two[:, 0:1]),
                              wait=w + [c_stq.tok()], inc=[(K.act, 1)])
            T["sgn"] = K.emit("scalar", lambda e: e.activation(qg[:], wpin[:], AF.Sign),
                              wait=[], inc=[(K.act, 1)])
            wpq_t[j] = T
            return T

        def wpq_slice_a(T, nb):
            bA, bB = rotc[0] % 4, (rotc[0] + 1) % 4
            rotc[0] += 2
            T["bk", nb] = (bA, bB)
            wk = [(K.act, T["abs"])]
            if bA in rg:
                wk.append(rg[bA])
            t_smm = K.emit("tensor",
                lambda e, nb=nb, bA=bA: e.matmul(ring[bA][0:2, :], t_E2[:],
                                                 wpo[:, nb * 512:(nb + 1) * 512], start=True, stop=True,
                                                 skip_group_check=True),
                wait=wk, inc=[(K.pe, 1)])
            rp = nb % 2
            wc = [(K.pe, t_smm)]
            if ("bmm", nb - 2) in T:
                wc.append((K.pe, T["bmm", nb - 2]))
            T["clip", nb] = K.emit("vector",
                lambda e, rp=rp, bA=bA: e.tensor_scalar_max(qsb[rp][:], ring[bA][0:2, :], 1e-8),
                wait=wc, inc=[(K.dve, 1)], rd=[], wr=[qsb[rp]])
            rg[bA] = (K.dve, T["clip", nb])

        def wpq_slice_b(T, nb):
            bA, bB = T["bk", nb]
            rp = nb % 2
            wb = [(K.dve, T["clip", nb])]
            if bB in rg:
                wb.append(rg[bB])
            t_bmm = K.emit("tensor",
                lambda e, rp=rp, bB=bB: e.matmul(ring[bB][:, :], t_B2[:], qsb[rp][:], start=True, stop=True,
                                                 skip_group_check=True),
                wait=wb, inc=[(K.pe, 1)])
            T["bmm", nb] = t_bmm
            T["ev"][nb] = K.emit("scalar",
                lambda e, nb=nb, bB=bB: e.activation(wpin[:, nb * 512:(nb + 1) * 512], ring[bB][:, :], AF.Identity),
                wait=[(K.pe, t_bmm)], inc=[(K.act, 1)])
            rg[bB] = (K.act, T["ev"][nb])

        def wpq_tail(T):
            j = T["j"]
            w = [(K.act, T["ev"][3]), (K.act, T["sgn"])]
            K.emit("vector", lambda e: e.tensor_tensor(wpo[:], wpo[:], wpin[:], ALU.is_gt),
                   wait=w, inc=[(K.dve, 1)], rd=[wpo, wpin], wr=[wpo])
            K.emit("vector", lambda e: e.tensor_tensor(wpo[:], wpo[:], wpin[:], ALU.mult),
                   wait=[], inc=[(K.dve, 1)], rd=[wpo, wpin], wr=[wpo])
            T["tt"] = K.emit("vector", lambda e: e.tensor_tensor(wpo[:], wpo[:], qg[:], ALU.mult),
                             wait=[], inc=[(K.dve, 1)], rd=[wpo, qg], wr=[wpo])
            K.emit("sync", lambda e, j=j: e.dma_start(out=d_wpq[j, :, :], in_=wpo[:]),
                   wait=[(K.dve, T["tt"])], inc=[(c_stq, 16)])

        def emit_gu_tb(m, tb):
            p = m % 3
            T = quant_t[m]
            pb = tb % 2
            def f_g(e, p=p, tb=tb, pb=pb):
                last = None
                for kcx in range(16):
                    last = e.matmul(psg[pb][:], wqb[p][:, kcx, 0:128], xpn[:, kcx, tb * 512:(tb + 1) * 512],
                                    start=(kcx == 0), stop=(kcx == 15))
                for kcx in range(16):
                    last = e.matmul(psu[pb][:], wqb[p][:, kcx, 128:256], xpn[:, kcx, tb * 512:(tb + 1) * 512],
                                    start=(kcx == 0), stop=(kcx == 15))
                return last
            w = [(K.dve, T["tt"]), pnq[tb // 2]]
            t_h = gu_h.setdefault(m, {})
            if tb >= 2:
                w.append((K.dve, t_h[tb - 2]))
            elif m - 1 in gu_h:
                w.append((K.dve, gu_h[m - 1][max(0, NTB - 2 + tb)]))
            t_mm = K.emit("tensor", f_g, wait=w, inc=[(K.pe, 1)])
            t_sil = K.emit("scalar", lambda e, pb=pb: e.activation(sil[pb][:], psg[pb][:], AF.Silu),
                           wait=[(K.pe, t_mm)], inc=[(K.act, 1)])
            t_h[tb] = K.emit("vector", lambda e, pb=pb: e.tensor_mul(hst[pb][:], sil[pb][:], psu[pb][:]),
                             wait=[(K.act, t_sil), c_sth[pb].tok()], inc=[(K.dve, 1)],
                             rd=[sil[pb]], wr=[hst[pb]])
            K.emit("sync", lambda e, m=m, tb=tb, pb=pb: e.dma_start(
                out=d_hs[m, :, tb * 512:(tb + 1) * 512], in_=hst[pb][:]),
                wait=[(K.dve, t_h[tb])], inc=[(c_sth[pb], 16)])
            quant_t[m]["lastmm"] = t_mm

        def ab_units(T, n, fa, fb):
            units = [lambda: fa(T, 0)]
            for kk in range(1, n):
                units.append(lambda kk=kk: fa(T, kk))
                units.append(lambda kk=kk - 1: fb(T, kk))
            units.append(lambda: fb(T, n - 1))
            return units

        for mm0 in (0, 1):
            T0 = quant_start(mm0)
            for u in ab_units(T0, 8, quant_slice_a, quant_slice_b):
                u()
            quant_tail(T0)
        for m in range(NPAIR):
            units = []
            if m + 2 < NPAIR:
                Tn = quant_start(m + 2)
                units += ab_units(Tn, 8, quant_slice_a, quant_slice_b)
                units.append(lambda Tn=Tn: quant_tail(Tn))
            Tw = wpq_start(m)
            units += ab_units(Tw, 4, wpq_slice_a, wpq_slice_b)
            units.append(lambda Tw=Tw: wpq_tail(Tw))
            for tb in range(NTB):
                emit_gu_tb(m, tb)
            for u in units:
                u()
        K.emit("sync", None, wait=[c_sth[0].tok(), c_sth[1].tok(), c_stq.tok()])
        K.barrier()

    def p6():
        nc.sbuf_base = base_sbuf
        nc.psum_base = base_psum
        NT6 = G // 4
        ht = sb("p6_ht", [128, 64, 512], BF16)
        h2r = [sb(f"p6_h2_{i}", [128, 4096], BF16) for i in range(2)]
        wpd = [sb(f"p6_wpd{i}", [128, 32, 512], BF16) for i in range(3)]
        stage = [sb("p6_st0", [128, 512], F32), sb("p6_st1", [128, 512], F32)]
        srow = sb("p6_srow", [1, 512], F32)
        srow2 = sb("p6_srow2", [1, 512], F32)
        rstd6 = sb("p6_rstd", [128, 4], F32)
        psr = ps("p6_psr", [128, 512], F32)
        psc = ps("p6_psc", [128, 512], F32)
        psp = [ps("p6_psp0", [128, 512], F32), ps("p6_psp1", [128, 512], F32)]
        c_ldh = K.ctr("p6_ldh")
        c_ldw = [K.ctr("p6_ldw0"), K.ctr("p6_ldw1"), K.ctr("p6_ldw2")]
        c_acc = [K.ctr("p6_acc0"), K.ctr("p6_acc1")]
        proj_done = {}
        slot_reader = {}
        prev_ev6 = None
        acc_tok = {}
        wslot = 0
        prev_p1last = prev_pmlast = None
        for tt in range(NT6):
            w = [prev_p1last] if prev_p1last is not None else []
            K.dma_batch("sync", c_ldh, [
                (lambda e, tt=tt, j8=j8: e.dma_start(
                    out=ht[:, j8 * 8:(j8 + 1) * 8, :],
                    in_=d_hs[j8 * 8:(j8 + 1) * 8, :, tt * 512:(tt + 1) * 512].rearrange("c p t -> p c t")))
                for j8 in range(4)], wait=w)
            ldhvA = c_ldh.tok()
            w = [prev_pmlast] if prev_pmlast is not None else []
            K.dma_batch("sync", c_ldh, [
                (lambda e, tt=tt, j8=j8: e.dma_start(
                    out=ht[:, j8 * 8:(j8 + 1) * 8, :],
                    in_=d_hs[j8 * 8:(j8 + 1) * 8, :, tt * 512:(tt + 1) * 512].rearrange("c p t -> p c t")))
                for j8 in range(4, 8)], wait=w)
            ldhv = c_ldh.tok()
            mm_toks = {}
            ssq_units = []
            def u_h2(hb8):
                rp = hb8 % 2
                w = [ldhvA if hb8 < 4 else ldhv]
                if hb8 - 2 in mm_toks:
                    w.append((K.pe, mm_toks[hb8 - 2]))
                T6[hb8] = K.emit("vector",
                    lambda e, hb8=hb8, rp=rp: e.tensor_mul(
                        h2r[rp].ap().rearrange("p (c t) -> p c t", c=8),
                        ht[:, hb8 * 8:(hb8 + 1) * 8, :],
                        ht[:, hb8 * 8:(hb8 + 1) * 8, :]),
                    wait=w, inc=[(K.dve, 1)], rd=[ht], wr=[h2r[rp]])
            def u_ssq(hb8):
                rp = hb8 % 2
                def f_ssq(e, hb8=hb8, rp=rp):
                    last = None
                    for jj in range(8):
                        last = e.matmul(psr[0:1, :], t_onebf[:],
                                        h2r[rp][:, jj * 512:(jj + 1) * 512],
                                        start=(hb8 == 0 and jj == 0), stop=(hb8 == 7 and jj == 7))
                    return last
                mm_toks[hb8] = K.emit("tensor", f_ssq, wait=[(K.dve, T6[hb8])], inc=[(K.pe, 1)])
            def u_rstd():
                va = K.act.reserve(1)
                def f_sq(e, va=va):
                    e.activation(srow[:], psr[0:1, :], AF.Ln, scale=t_i8k[:], bias=t_epsf[:]).then_inc(K.act.h, 1)
                    e.wait_ge(K.act.h, va)
                    return e.activation(srow2[:], srow[:], AF.Exp, scale=t_nh1[:, 0:1])
                t_sq = K.emit("scalar", f_sq, wait=[(K.pe, mm_toks[7])], inc=[(K.act, 1)])
                def f_cols(e):
                    last = None
                    for j in range(4):
                        last = e.matmul(psc[:, j:j + 1], srow2[0:1, j * 128:(j + 1) * 128], t_onef[:],
                                        start=(j == 0), stop=(j == 3), skip_group_check=True)
                    return last
                t_cols = K.emit("tensor", f_cols, wait=[(K.act, t_sq)], inc=[(K.pe, 1)])
                tr = K.emit("vector", lambda e: e.tensor_copy(rstd6[:], psc[:, 0:4]),
                            wait=[(K.pe, t_cols)], inc=[(K.dve, 1)], rd=[], wr=[rstd6])
                t_revs.append(tr)
            T6 = {}
            t_revs = []
            for hb8 in range(8):
                ssq_units.append(lambda hb8=hb8: u_h2(hb8))
                ssq_units.append(lambda hb8=hb8: u_ssq(hb8))
            ssq_units.append(u_rstd)
            sui = [0]
            def drain_unit(n=1):
                for _ in range(n):
                    if sui[0] < len(ssq_units):
                        ssq_units[sui[0]]()
                        sui[0] += 1
            for d in range(4):
                for half in range(2):
                    slot = wslot % 3
                    w = [slot_reader[slot]] if slot in slot_reader else []
                    K.dma_batch("sync", c_ldw[slot], [
                        (lambda e, d=d, half=half, slot=slot, j4=j4: e.dma_start(
                            out=wpd[slot][:, j4 * 8:(j4 + 1) * 8, :],
                            in_=d_wpq[half * 32 + j4 * 8:half * 32 + (j4 + 1) * 8, :, d * 512:(d + 1) * 512]
                            .rearrange("c p n -> p c n")))
                        for j4 in range(4)], wait=w)
                    wslot += 1
                sA, sB = (wslot - 2) % 3, (wslot - 1) % 3
                ev6 = {}
                t_pm = None
                for t1 in range(4):
                    drain_unit(3)
                    pb = t1 % 2
                    def f_p1(e, t1=t1, pb=pb, sA=sA):
                        last = None
                        for hf in range(32):
                            last = e.matmul(psp[pb][:], ht[:, hf, t1 * 128:(t1 + 1) * 128],
                                            wpd[sA][:, hf, :], start=(hf == 0), stop=False)
                        return last
                    def f_p2(e, t1=t1, pb=pb, sB=sB):
                        last = None
                        for hf in range(32, 64):
                            last = e.matmul(psp[pb][:], ht[:, hf, t1 * 128:(t1 + 1) * 128],
                                            wpd[sB][:, hf - 32, :], start=False, stop=(hf == 63))
                        return last
                    w = [c_ldw[sA].tok(), ldhvA]
                    if t1 >= 2:
                        w.append((K.act, ev6[t1 - 2]))
                    elif prev_ev6 is not None:
                        w.append((K.act, prev_ev6[t1 + 2]))
                    t_p1 = K.emit("tensor", f_p1, wait=w, inc=[(K.pe, 1)])
                    t_pm = K.emit("tensor", f_p2, wait=[c_ldw[sB].tok(), ldhv], inc=[(K.pe, 1)])
                    if d == 3 and t1 == 3:
                        prev_p1last = (K.pe, t_p1)
                        prev_pmlast = (K.pe, t_pm)
                    w = [(K.pe, t_pm)]
                    if t1 >= 2:
                        key = (tt, d, t1 - 2)
                    elif d > 0:
                        key = (tt, d - 1, t1 + 2)
                    else:
                        key = (tt - 1, 3, t1 + 2)
                    if key in acc_tok:
                        w.append((c_acc[pb], acc_tok[key]))
                    while sui[0] < len(ssq_units):
                        drain_unit(1)
                    w.append((K.dve, t_revs[0]))
                    ev6[t1] = K.emit("scalar",
                        lambda e, t1=t1, pb=pb: e.activation(stage[pb][:], psp[pb][:], AF.Identity,
                                                             scale=rstd6[:, t1:t1 + 1]),
                        wait=w, inc=[(K.act, 1)])
                    K.emit("gpsimd", lambda e, tt=tt, t1=t1, d=d, pb=pb: e.dma_start(
                        out=d_out[tt * 512 + t1 * 128:tt * 512 + (t1 + 1) * 128, d * 512:(d + 1) * 512],
                        in_=stage[pb][:], accum_op=ALU.add),
                        wait=[(K.act, ev6[t1])], inc=[(c_acc[pb], 16)])
                    acc_tok[(tt, d, t1)] = c_acc[pb].n
                slot_reader[sA] = (K.pe, t_pm)
                slot_reader[sB] = (K.pe, t_pm)
                prev_ev6 = ev6
            proj_done[tt] = prev_ev6[3]
        K.emit("gpsimd", None, wait=[c_acc[0].tok(), c_acc[1].tok()])
        K.barrier()

    allp = {"p0": p0, "q": lambda: p_qkv("q"), "k": lambda: p_qkv("k"),
            "v": lambda: p_qkv("v"), "px": px, "p4": p4, "p5": p5, "p6": p6}
    order = ["p0", "q", "k", "v", "px", "p4", "p5", "p6"]
    for name in order:
        if phases is None or name in phases:
            allp[name]()
    K.replay()
    return nc



def host_tables(core, G):
    b, s = core // 4, core % 4
    Tc = G * 128
    off = s * Tc
    f32 = np.float32
    r = _slopes_r()
    i128 = np.arange(128, dtype=f32)
    rdecay = np.power(r[None, :], (i128[:, None] + f32(1.0))).astype(f32)
    kd = np.power(r[:, None], (f32(127.0) - i128[None, :])).astype(f32)
    kdecay = np.repeat(kd.T[:, :, None], HD, axis=2).reshape(128, E).astype(f32)
    diff = i128[None, :] - i128[:, None]
    dmaskT = np.where(diff[None] >= 0,
                      np.power(r[:, None, None], np.where(diff >= 0, diff, 0)[None]),
                      f32(0.0)).astype(f32)
    lin = np.linspace(0.0, 1.0, HD // 4, dtype=f32)
    freq = np.power(f32(1.0 / 1024.0), lin).astype(f32)
    t = (off + np.arange(Tc, dtype=np.float64)).astype(f32)
    theta = t[:, None] * freq[None, :]
    costab = np.tile(np.cos(theta).astype(f32).reshape(G, 128, 32), (1, 1, 8))
    sintab = np.tile(np.sin(theta).astype(f32).reshape(G, 128, 32), (1, 1, 8))
    wv = np.zeros((8, NH), dtype=f32)
    for j in range(8):
        bj, sj = j // 4, j % 4
        if bj == b and sj < s:
            wv[j] = np.power(r, f32(Tc * (s - 1 - sj))).astype(f32)
    wvec = np.ascontiguousarray(np.broadcast_to(wv[None], (128, 8, NH)))
    E2t = np.zeros((128, 2), dtype=ml_dtypes.bfloat16)
    E2t[:64, 0] = ml_dtypes.bfloat16(1.0 / 128.0)
    E2t[64:, 1] = ml_dtypes.bfloat16(1.0 / 128.0)
    B2t = np.zeros((2, 128), dtype=ml_dtypes.bfloat16)
    B2t[0, :64] = ml_dtypes.bfloat16(1.0)
    B2t[1, 64:] = ml_dtypes.bfloat16(1.0)
    return dict(costab=np.ascontiguousarray(costab), sintab=np.ascontiguousarray(sintab),
                kdecay=np.ascontiguousarray(kdecay), dmaskT=np.ascontiguousarray(dmaskT),
                rdecay=np.ascontiguousarray(rdecay), wvec=wvec, E2=E2t, B2=B2t,
                id32=np.eye(128, dtype=f32), idbf=np.eye(128, dtype=ml_dtypes.bfloat16))


def make_in_maps(x, wq, wk, wv, wo, rms_w, w_gate_up, w_proj, G):
    Tc = G * 128
    wguT = np.ascontiguousarray(np.asarray(w_gate_up).T.astype(np.float32))
    wpT = np.ascontiguousarray(np.asarray(w_proj).T.astype(np.float32))
    rmsw = np.ascontiguousarray(np.asarray(rms_w).astype(np.float32).reshape(16, 128).T)
    in_maps = []
    for core in range(8):
        b, s = core // 4, core % 4
        m = {"x": np.ascontiguousarray(x[b, s * Tc:(s + 1) * Tc]),
             "wq": np.asarray(wq), "wk": np.asarray(wk), "wv": np.asarray(wv),
             "wo": np.asarray(wo), "rmsw": rmsw, "wguT": wguT, "wpT": wpT}
        m.update(host_tables(core, G))
        in_maps.append(m)
    return in_maps


def run_kernel(x, wq, wk, wv, wo, rms_w, w_gate_up, w_proj, G, dbg=False, nc=None, phases=None):
    Tc = G * 128
    T = 4 * Tc
    assert x.shape == (2, T, E)
    if nc is None:
        nc = build_nc(G, dbg, phases=phases)
    in_maps = make_in_maps(x, wq, wk, wv, wo, rms_w, w_gate_up, w_proj, G)
    res = run_bass_kernel_spmd(nc, in_maps, core_ids=list(range(8)))
    out = np.zeros((2, T, E), dtype=np.float32)
    for core in range(8):
        b, s = core // 4, core % 4
        out[b, s * Tc:(s + 1) * Tc] = res.results[core]["out"]
    return out, res



def kernel(L0_attn_x, L0_attn_Wq, L0_attn_Wk, L0_attn_Wv, L0_attn_Wo,
           L0_attn_rms_w, L0_ffn_w_gate_up, L0_ffn_w_proj):
    G = 32
    x = np.asarray(L0_attn_x)
    out, _res = run_kernel(x, np.asarray(L0_attn_Wq), np.asarray(L0_attn_Wk),
                           np.asarray(L0_attn_Wv), np.asarray(L0_attn_Wo),
                           np.asarray(L0_attn_rms_w), np.asarray(L0_ffn_w_gate_up),
                           np.asarray(L0_ffn_w_proj), G)
    return out
